# version 03 (final)
# baseline (speedup 1.0000x reference)
import numpy as np
import concourse.bass as bass
import concourse.mybir as mybir
from concourse.bass_utils import run_bass_kernel_spmd
from concourse.bass_isa import ReduceOp
U8 = mybir.dt.uint8

F32 = mybir.dt.float32
BF16 = mybir.dt.bfloat16
AF = mybir.ActivationFunctionType
ALU = mybir.AluOpType
AX = mybir.AxisListType

T, D, PT = 2048, 1024, 128
NT, NCH = T // PT, D // PT
NCORES = 8
C23 = float(1.5 * 2.0**23)
C52 = float(1.5 * 2.0**52)
NEG_BIG = -1e30
ENGINES = ("tensor", "vector", "scalar", "gpsimd", "sync")

N_LAYERS = 4
LAST_EXTRA = {}
STAGES = ("attn", "proj", "ffn")
DBG = ()


class Prog:
    def __init__(self, nc):
        self.nc = nc
        self.q = {e: [] for e in ENGINES}
        self.channels = []
        self.ch_count = []

    def channel(self, queue):
        self.channels.append(queue)
        self.ch_count.append(0)
        return len(self.channels) - 1

    def op(self, eng, fn, deps=()):
        idx = len(self.q[eng])
        self.q[eng].append(("c", fn, tuple(d for d in deps if d is not None)))
        return ("c", eng, idx)

    def dma(self, ch, fn, deps=()):
        queue = self.channels[ch]
        n = self.ch_count[ch]
        self.ch_count[ch] += 1
        self.q[queue].append(("d", fn, tuple(d for d in deps if d is not None), ch, n))
        return ("d", ch, n)

    def emit(self, block, esem, csem):
        needed = set()
        for e in ENGINES:
            for item in self.q[e]:
                for dep in item[2]:
                    needed.add(dep)
        cum = {}
        for e in ENGINES:
            c = 0
            for i, item in enumerate(self.q[e]):
                if item[0] == "c" and ("c", e, i) in needed:
                    c += 1
                    cum[("c", e, i)] = c
        dma_cum = {}
        for ci, cnt in enumerate(self.ch_count):
            for n in range(cnt):
                dma_cum[("d", ci, n)] = 16 * (n + 1)

        def make_emitter(e):
            def emitter(eng):
                last_wait = {}
                for i, item in enumerate(self.q[e]):
                    kind, fn, deps = item[0], item[1], item[2]
                    for dep in deps:
                        if dep[0] == "c":
                            de = dep[1]
                            if de == e and kind == "c" and e in ("tensor", "sync"):
                                continue
                            key, sem, val = ("c", de), esem[de], cum[dep]
                        else:
                            key, sem, val = ("d", dep[1]), csem[dep[1]], dma_cum[dep]
                        if last_wait.get(key, -1) < val:
                            eng.wait_ge(sem, val)
                            last_wait[key] = val
                    if kind == "d" and item[4] > 0:
                        key = ("d", item[3])
                        val = 16 * item[4]
                        if last_wait.get(key, -1) < val:
                            eng.wait_ge(csem[item[3]], val)
                            last_wait[key] = val
                    inst = fn(eng)
                    if kind == "c":
                        if ("c", e, i) in needed:
                            assert inst is not None, f"{e}[{i}] depended-on op returned None"
                            inst.then_inc(esem[e], 1)
                    else:
                        inst.then_inc(csem[item[3]], 16)
            return emitter

        for e in ENGINES:
            if self.q[e]:
                getattr(block, e)(make_emitter(e))


class Ctx:
    pass


def emit_fp4(cx, out_bf_ap, in_f32_ap, nblk, deps, ab_ap, m_ap):
    p, ce = cx.p, cx.ceff
    n = nblk * 16
    x3 = in_f32_ap.rearrange("p (a b) -> p a b", b=16)
    bm = cx.fp4_bm[:, 0:nblk]
    rinv = cx.fp4_rinv[:, 0:nblk]
    q1, q2, ab, m = cx.sB[:, 0:n], cx.sC[:, 0:n], ab_ap, m_ap
    e_bm = p.op("vector", lambda v: v.tensor_reduce(bm, x3, axis=AX.X, op=ALU.max, apply_absolute_value=True), deps=deps)
    e_sc = p.op("vector", lambda v: v.tensor_scalar(bm, bm, 1.0 / 6.0, 1e-12, ALU.mult, ALU.max), deps=[e_bm])
    e_ri = p.op("vector", lambda v: v.reciprocal(rinv, bm), deps=[e_sc])
    rb = rinv.unsqueeze(2).broadcast_to((PT, nblk, 16))
    e_xn = p.op("vector", lambda v: v.tensor_tensor(x3, x3, rb, ALU.mult), deps=[e_ri])
    e_ab = p.op("scalar", lambda s: s.activation(ab, in_f32_ap, AF.Abs), deps=[e_xn])
    e_q1a = p.op("vector", lambda v: v.tensor_scalar(q1, in_f32_ap, 2.0, ce[:, 0:1], ALU.mult, ALU.add), deps=[e_xn])
    e_q1 = p.op("vector", lambda v: v.tensor_scalar(q1, q1, ce[:, 0:1], 0.5, ALU.subtract, ALU.mult), deps=[e_q1a])
    e_q2a = p.op("vector", lambda v: v.tensor_scalar(q2, in_f32_ap, ce[:, 0:1], None, ALU.add), deps=[e_xn])
    e_q2 = p.op("vector", lambda v: v.tensor_scalar(q2, q2, ce[:, 0:1], None, ALU.subtract), deps=[e_q2a])
    e_m1 = p.op("vector", lambda v: v.tensor_scalar(m, ab, 2.0, None, ALU.is_ge), deps=[e_ab])
    e_s1 = p.op("vector", lambda v: v.select(q1, m, q2, q1), deps=[e_m1, e_q1, e_q2])
    e_q3a = p.op("vector", lambda v: v.tensor_scalar(q2, in_f32_ap, 0.5, ce[:, 0:1], ALU.mult, ALU.add), deps=[e_s1])
    e_q3 = p.op("vector", lambda v: v.tensor_scalar(q2, q2, ce[:, 0:1], 2.0, ALU.subtract, ALU.mult), deps=[e_q3a])
    e_m2 = p.op("vector", lambda v: v.tensor_scalar(m, ab, 4.0, None, ALU.is_ge), deps=[e_s1])
    e_s2 = p.op("vector", lambda v: v.select(q1, m, q2, q1), deps=[e_m2, e_q3])
    bmb = bm.unsqueeze(2).broadcast_to((PT, nblk, 16))
    q13 = q1.rearrange("p (a b) -> p a b", b=16)
    e_dq = p.op("vector", lambda v: v.tensor_tensor(q13, q13, bmb, ALU.mult), deps=[e_s2])
    return p.op("vector", lambda v: v.tensor_copy(out_bf_ap, q1), deps=[e_dq])


def build_program(n_layers=N_LAYERS):
    import contextlib
    stack = contextlib.ExitStack()
    nc = bass.Bass("TRN2", target_bir_lowering=False)
    p = Prog(nc)
    cx = Ctx()
    cx.nc, cx.p, cx.stack = nc, p, stack

    def sb(name, shape, dtype=F32):
        return stack.enter_context(nc.sbuf_tensor(name, list(shape), dtype))

    def psum(name, shape, dtype=F32):
        return stack.enter_context(nc.psum_tensor(name, list(shape), dtype))
    cx.sb, cx.psum = sb, psum

    dram = {}
    def din(name, shape, dtype=F32):
        dram[name] = nc.dram_tensor(name, list(shape), dtype, kind="ExternalInput").ap()
        return dram[name]
    cx.din, cx.dram = din, dram

    din("x", (T, D))
    cx.y_out = nc.dram_tensor("y", [T, D], F32, kind="ExternalOutput").ap()
    din("c_eye", (128, 128))
    din("c_J", (128, 128))
    din("c_blk", (128, 2))
    din("c_ones", (128, 16))
    din("c_wline", (1, 2048))
    din("c_mline", (1, 6144))
    din("c_triu", (128, 128))
    din("c_one128", (128, 128))
    dram["c_msk"] = nc.dram_tensor("c_msk", [128, 256], mybir.dt.uint8, kind="ExternalInput").ap()
    cx.scr_dram = nc.dram_tensor("scr_dram", [64, 4096], F32).ap()
    cx.cc_i, cx.cc_o = [], []
    for r, shp in ((1, [1, 2]), (2, [1, 2]), (3, [1, 2]), (4, [1, 2])):
        ti = nc.dram_tensor(f"cc_i{r}", shp, F32)
        to = nc.dram_tensor(f"cc_o{r}", shp, F32)
        cx.cc_i.append(ti)
        cx.cc_o.append(to)
        dram[f"cc_i{r}"] = ti.ap()
        dram[f"cc_o{r}"] = to.ap()
    cx.scr_g = nc.dram_tensor("scr_g", [128, 49152], BF16).ap()
    cx.scr_u = nc.dram_tensor("scr_u", [128, 49152], BF16).ap()
    cx.scr_o = nc.dram_tensor("scr_o", [128, 49152], BF16).ap()

    din("l0_wq", (D, D)); din("l0_wk", (D, D)); din("l0_wv", (D, D)); din("l0_wo", (D, D))
    din("l0_bo", (1, D)); din("l0_rpb", (16, 1024))
    din("l0_wg", (2816, D)); din("l0_wu", (2816, D)); din("l0_wd", (D, 2816))
    din("l1_wq", (D, D)); din("l1_wk", (D, D)); din("l1_wv", (D, D)); din("l1_wout", (D, D))
    din("l1_wgq", (D, 512)); din("l1_wgk", (D, 512)); din("l1_wgv", (D, 512)); din("l1_rout", (512, D))
    din("l1_glw", (D, 2))
    din("l1_wg", (3072, D)); din("l1_wu", (3072, D)); din("l1_wd", (D, 3072))
    din("l2_wq", (D, D)); din("l2_wk", (D, D)); din("l2_wv", (D, D))
    din("l2_wa", (D, D)); din("l2_war", (D, D)); din("l2_wo", (D, D))
    din("l2_wb", (D, 16)); din("l2_wg", (D, 16))
    din("l2_ba", (128, 8)); din("l2_bar", (128, 8))
    din("l2_bb", (16, 1)); din("l2_bg", (16, 1))
    din("l2_lnw", (128, 8)); din("l2_lnb", (128, 8))
    din("l2_w1", (D, 4096)); din("l2_b1", (128, 32))
    din("l2_w2", (4096, D)); din("l2_b2", (1, D))
    din("l3_wq", (D, D)); din("l3_wk", (D, D)); din("l3_wv", (D, D)); din("l3_wo", (D, D))
    din("l3_bo", (1, D)); din("l3_lng", (128, 8)); din("l3_lnb", (128, 8))
    din("l3_wff", (D, D)); din("l3_ws", (1, D)); din("l3_bff", (1, D))

    cx.ch_w = p.channel("sync")
    cx.ch_wg = p.channel("gpsimd")
    cx.ch_x = p.channel("gpsimd")
    cx.ch_m = p.channel("scalar")
    cx.ch_s = p.channel("sync")

    cx.x_sb = sb("x_sb", [PT, NT, D], F32)
    cx.xnf = sb("xnf", [PT, NCH, T], BF16)
    cx.AR1 = sb("AR1", [PT, 16384], BF16)
    cx.AR2 = sb("AR2", [PT, 16384], BF16)
    cx.AR3 = sb("AR3", [PT, 12800], BF16)
    cx.sA = sb("sA", [PT, 1472], F32)
    cx.sB = sb("sB", [PT, 1472], F32)
    cx.sC = sb("sC", [PT, 1472], F32)
    cx.ln_stats = sb("ln_stats", [PT, 4 * NT], F32)
    cx.eye_bf = sb("eye_bf", [PT, 128], BF16)
    cx.J_bf = sb("J_bf", [PT, 128], BF16)
    cx.blk_bf = sb("blk_bf", [PT, 2], BF16)
    cx.ones_f = sb("ones_f", [PT, 16], F32)
    cx.ceff = sb("ceff", [PT, 8], F32)
    cx.kap = sb("kap", [PT, 8], F32)
    cx.fp4_bm = sb("fp4_bm", [PT, 96], F32)
    cx.fp4_rinv = sb("fp4_rinv", [PT, 96], F32)
    cx.small = sb("small", [PT, 64], F32)
    cx.rpb_sb = sb("rpb_sb", [16, 1024], BF16)
    cx.msk_sb = sb("msk_sb", [PT, 1024], U8)
    cx.pb = [psum(f"pb{i}", [PT, 512], F32) for i in range(6)]
    cx.pbt = [psum(f"pbt{i}", [PT, 1024], BF16) for i in range(2)]

    cx.ev_eye = p.dma(cx.ch_wg, lambda e: e.dma_start(out=cx.eye_bf[:, :], in_=dram["c_eye"]))
    cx.ev_J = p.dma(cx.ch_wg, lambda e: e.dma_start(out=cx.J_bf[:, :], in_=dram["c_J"]))
    cx.ev_blk = p.dma(cx.ch_wg, lambda e: e.dma_start(out=cx.blk_bf[:, :], in_=dram["c_blk"]))
    ev_ones = p.dma(cx.ch_m, lambda e: e.dma_start(out=cx.ones_f[:, :], in_=dram["c_ones"]))
    cx.ev_x = p.dma(cx.ch_x, lambda e: e.dma_start(
        out=cx.x_sb[:, :, :], in_=dram["x"].rearrange("(t p) d -> p t d", p=PT)))

    ek1 = p.op("vector", lambda v: v.tensor_scalar(cx.kap[:, :], cx.ones_f[:, 0:8], C52, None, ALU.add), deps=[ev_ones])
    ek2 = p.op("vector", lambda v: v.tensor_scalar(cx.kap[:, :], cx.kap[:, :], C52, None, ALU.subtract), deps=[ek1])
    cx.ev_ceff = p.op("vector", lambda v: v.tensor_scalar(cx.ceff[:, :], cx.kap[:, :], float(C52 - C23), C23, ALU.mult, ALU.add), deps=[ek2])
    cx.ev_ones = ev_ones

    cx.dbg_evs = []
    cx.bw = {}
    ln_evs, xnf_evs = emit_ln(cx, [cx.ev_x, cx.ev_ceff])
    if "lnonly" in STAGES:
        if "xnf" in DBG:
            dx = nc.dram_tensor("y_xnf", [PT, NCH * T], BF16, kind="ExternalOutput").ap()
            cx.dbg_evs.append(p.dma(cx.ch_m, lambda e: e.dma_start(out=dx, in_=cx.xnf[:, :, :].rearrange("p c t -> p (c t)")), deps=list(xnf_evs)))
        if "stats" in DBG:
            ds = nc.dram_tensor("y_stats", [PT, 2 * NT], F32, kind="ExternalOutput").ap()
            cx.dbg_evs.append(p.dma(cx.ch_m, lambda e: e.dma_start(out=ds, in_=cx.ln_stats[:, :]), deps=list(ln_evs)))
        x_evs = list(xnf_evs)
    else:
        x_evs = emit_layer0(cx, ln_evs, xnf_evs)
        if "l1" in STAGES or (isinstance(n_layers, int) and n_layers >= 2):
            ln_evs1, xnf_evs1 = emit_ln(cx, x_evs)
            x_evs = emit_layer1(cx, ln_evs1, xnf_evs1)
        if "l2" in STAGES or (isinstance(n_layers, int) and n_layers >= 3):
            ln_evs2, xnf_evs2 = emit_ln(cx, x_evs)
            x_evs = emit_layer2(cx, ln_evs2, xnf_evs2)
        if "l3" in STAGES or (isinstance(n_layers, int) and n_layers >= 4):
            ln_evs3, xnf_evs3 = emit_ln(cx, x_evs)
            if "ln3" in DBG:
                ds3 = nc.dram_tensor("y_st3", [PT, 2 * NT], F32, kind="ExternalOutput").ap()
                cx.dbg_evs.append(p.dma(cx.ch_m, lambda e: e.dma_start(out=ds3, in_=cx.ln_stats[:, :]), deps=list(ln_evs3)))
                dx3 = nc.dram_tensor("y_xnf3", [PT, NCH * T], BF16, kind="ExternalOutput").ap()
                cx.dbg_evs.append(p.dma(cx.ch_m, lambda e: e.dma_start(out=dx3, in_=cx.xnf[:, :, :].rearrange("p c t -> p (c t)")), deps=list(xnf_evs3)))
            x_evs = emit_layer3(cx, ln_evs3, xnf_evs3)

    ev_y = p.dma(cx.ch_x, lambda e: e.dma_start(
        out=cx.y_out.rearrange("(t p) d -> p t d", p=PT), in_=cx.x_sb[:, :, :]), deps=x_evs)
    p.op("sync", lambda e: e.nop(), deps=[ev_y] + cx.dbg_evs)
    return cx


def emit_ln(cx, deps, double=False, dst=None, st_off=0):
    p = cx.p
    st = cx.ln_stats
    if dst is None:
        dst = cx.xnf
    stat_evs, xnf_evs = [], []
    prev_war = []
    for t in range(NT):
        xt = cx.x_sb[:, t, :]
        rstd = st[:, st_off + 2 * t:st_off + 2 * t + 1]
        nmr = st[:, st_off + 2 * t + 1:st_off + 2 * t + 2]
        mu = cx.small[:, 0:1]
        var = cx.small[:, 1:2]
        xn = cx.sA[:, 0:D]
        e1 = p.op("vector", lambda v, xt=xt, mu=mu: v.tensor_reduce(mu, xt, axis=AX.X, op=ALU.add), deps=list(deps) + prev_war)
        e1b = p.op("vector", lambda v, mu=mu: v.tensor_scalar(mu, mu, 1.0 / D, None, ALU.mult), deps=[e1])
        e2 = p.op("vector", lambda v, xn=xn, xt=xt, mu=mu: v.tensor_scalar(xn, xt, mu[:, 0:1], None, ALU.subtract), deps=[e1b])
        e3 = p.op("scalar", lambda s, xn=xn, var=var: s.activation(cx.sB[:, 0:D], xn, AF.Square, accum_out=var), deps=[e2])
        e3b = p.op("vector", lambda v, var=var: v.tensor_scalar(var, var, 1.0 / D, 1e-5, ALU.mult, ALU.add), deps=[e3])
        e4 = p.op("scalar", lambda s, var=var, rstd=rstd: s.activation(rstd, var, AF.Sqrt), deps=[e3b])
        e5 = p.op("vector", lambda v, rstd=rstd: v.reciprocal(rstd, rstd), deps=[e4])
        if double:
            t2 = cx.small[:, 2:3]
            ed1 = p.op("vector", lambda v, rstd=rstd, t2=t2: v.tensor_tensor(t2, rstd, rstd, ALU.mult), deps=[e5])
            ed2 = p.op("vector", lambda v, t2=t2: v.tensor_scalar(t2, t2, -1e-5, 1.0 + 1e-5, ALU.mult, ALU.add), deps=[ed1])
            ed3 = p.op("scalar", lambda s, t2=t2: s.activation(t2, t2, AF.Sqrt), deps=[ed2])
            ed4 = p.op("vector", lambda v, t2=t2: v.reciprocal(t2, t2), deps=[ed3])
            e5 = p.op("vector", lambda v, rstd=rstd, t2=t2: v.tensor_tensor(rstd, rstd, t2, ALU.mult), deps=[ed4])
        e5b = p.op("vector", lambda v, mu=mu, rstd=rstd, nmr=nmr: v.tensor_tensor(nmr, mu, rstd, ALU.mult), deps=[e5, e1b])
        e5c = p.op("vector", lambda v, nmr=nmr: v.tensor_scalar(nmr, nmr, -1.0, None, ALU.mult), deps=[e5b])
        stat_evs.append(e5c)
        xb = cx.sB[:, 448:1472].bitcast(BF16)[:, 0:D]
        e6 = p.op("vector", lambda v, xb=xb, xt=xt, rstd=rstd, nmr=nmr: v.tensor_scalar(
            xb, xt, rstd[:, 0:1], nmr[:, 0:1], ALU.mult, ALU.add), deps=[e5c, e3])
        tevs = []
        for c in range(NCH):
            bank = cx.pbt[c % 2]
            e_t = p.op("tensor", lambda tt, c=c, bank=bank, xb=xb: tt.transpose(
                bank[:, 0:128], xb[:, c * 128:(c + 1) * 128], cx.eye_bf[:, :]),
                deps=[e6, cx.ev_eye] + ([tevs[-2][1]] if len(tevs) >= 2 else [cx.bw.get(("t", c % 2))]))
            e_c = p.op("vector", lambda v, t=t, c=c, bank=bank: v.tensor_copy(
                dst[:, c, t * 128:(t + 1) * 128], bank[:, 0:128]), deps=[e_t])
            tevs.append((e_t, e_c))
        xnf_evs.extend(ec for _, ec in tevs)
        prev_war = [e3, e6, tevs[-1][1]]
    cx.bw[("t", 0)] = tevs[-2][1]
    cx.bw[("t", 1)] = tevs[-1][1]
    return stat_evs, xnf_evs


def emit_layer0(cx, stat_evs, xnf_evs):
    p, d = cx.p, cx.dram
    A1, A2, A3 = cx.AR1, cx.AR2, cx.AR3
    o_tm = A1[:, 0:16384].rearrange("p (t w) -> p t w", t=NT)
    wqkv = A2[:, 0:3072].rearrange("p (c w) -> p c w", c=NCH)
    qf = A2[:, 3072:5120]
    kf = A2[:, 5120:7168]
    vt = A2[:, 7168:8256].rearrange("p (t w) -> p t w", t=NT)
    vt2 = A2[:, 8256:9344].rearrange("p (t w) -> p t w", t=NT)
    strips = A2[:, 9344:14976].rearrange("p (h w) -> p h w", h=2)
    pP = [A3[:, i * 512:(i + 1) * 512] for i in range(4)]
    sq_bf = A3[:, 2048:4096]
    negc = cx.small[:, 40:41]

    ev_rpb = p.dma(cx.ch_wg, lambda e: e.dma_start(out=cx.rpb_sb[:, :], in_=d["l0_rpb"]))

    r = cx.sA[0:16, 0:1472]
    eb0 = p.op("vector", lambda v: v.tensor_scalar(r[:, 0:511], cx.rpb_sb[:, 0:511], 0.0, NEG_BIG, ALU.mult, ALU.add), deps=[ev_rpb] + list(stat_evs))
    eb1 = p.op("vector", lambda v: v.tensor_scalar(r[:, 511:1472], cx.rpb_sb[:, 1023:62:-1], 1.0 / 64.0, None, ALU.mult), deps=[ev_rpb] + list(stat_evs))
    ev_bl0 = p.dma(cx.ch_s, lambda e: e.dma_start(out=cx.scr_dram[0:16, 0:1472], in_=r[:, :]), deps=[eb0, eb1])
    eb2 = p.op("vector", lambda v: v.tensor_scalar(r[:, 0:63], cx.rpb_sb[:, 62::-1], 1.0 / 64.0, None, ALU.mult), deps=[ev_bl0])
    eb3 = p.op("vector", lambda v: v.tensor_scalar(r[:, 63:1472], cx.rpb_sb[:, 0:1].broadcast_to((16, 1409)), 1.0 / 64.0, None, ALU.mult), deps=[ev_bl0])
    ev_bl = p.dma(cx.ch_s, lambda e: e.dma_start(out=cx.scr_dram[0:16, 1472:2944], in_=r[:, :]), deps=[eb2, eb3])

    e_r1 = p.op("vector", lambda v: v.tensor_reduce(cx.small[0:16, 33:34], cx.rpb_sb[:, :], axis=AX.X, op=ALU.max, apply_absolute_value=True), deps=[ev_rpb])
    mb_bf = A3[0:16, 2048:2049]
    e_r1b = p.op("vector", lambda v: v.tensor_copy(mb_bf, cx.small[0:16, 33:34]), deps=[e_r1])
    e_r1t = p.op("tensor", lambda tt: tt.transpose(
        cx.pbt[0][:, 0:16], mb_bf.broadcast_to((16, 128)), cx.eye_bf[0:16, 0:16]), deps=[e_r1b, cx.ev_eye])
    e_r2 = p.op("vector", lambda v: v.tensor_reduce(cx.small[:, 34:35], cx.pbt[0][:, 0:16], axis=AX.X, op=ALU.max), deps=[e_r1t])
    _init_t0_war = e_r2

    x_upd = []
    prev_use = []
    last_bank_user = cx.bw
    last_bank_user[("t", 0)] = _init_t0_war
    for hp in range(1 if "hp1" in STAGES else NCH):
        hA, hB = 2 * hp, 2 * hp + 1
        strip_ev = []
        for hh, h in enumerate((hA, hB)):
            strip_ev.append(p.dma(cx.ch_wg, lambda e, h=h, hh=hh: e.dma_start(
                out=strips[:, hh, :],
                in_=bass.AP(cx.scr_dram.tensor, h * 4096, [[1, 128], [1, 2816]])),
                deps=[ev_bl] + prev_use))
        wevs = []
        for c in range(NCH):
            e_q = p.dma(cx.ch_wg, lambda e, c=c, hp=hp: e.dma_start(
                out=wqkv[:, c, 0:128], in_=d["l0_wq"][c * 128:(c + 1) * 128, hp * 128:(hp + 1) * 128]), deps=prev_use)
            e_k = p.dma(cx.ch_wg, lambda e, c=c, hp=hp: e.dma_start(
                out=wqkv[:, c, 128:256], in_=d["l0_wk"][c * 128:(c + 1) * 128, hp * 128:(hp + 1) * 128]), deps=prev_use)
            e_v = p.dma(cx.ch_wg, lambda e, c=c, hp=hp: e.dma_start(
                out=wqkv[:, c, 256:384], in_=d["l0_wv"][c * 128:(c + 1) * 128, hp * 128:(hp + 1) * 128]), deps=prev_use)
            wevs.append((e_q, e_k, e_v))

        qk_done, v_done = [], []
        for tq in range(4):
            bq, bk = cx.pb[tq % 2], cx.pb[2 + tq % 2]
            mq = mk = None
            for c in range(NCH):
                st_, sp = (c == 0), (c == NCH - 1)
                war_q = [last_bank_user.get(tq % 2)] if c == 0 else []
                war_k = [last_bank_user.get(2 + tq % 2)] if c == 0 else []
                mq = p.op("tensor", lambda tt, c=c, tq=tq, st_=st_, sp=sp, bq=bq: tt.matmul(
                    bq[:, 0:512], wqkv[:, c, 0:128], cx.xnf[:, c, tq * 512:(tq + 1) * 512], start=st_, stop=sp),
                    deps=[wevs[c][0]] + list(xnf_evs) + war_q)
                mk = p.op("tensor", lambda tt, c=c, tq=tq, st_=st_, sp=sp, bk=bk: tt.matmul(
                    bk[:, 0:512], wqkv[:, c, 128:256], cx.xnf[:, c, tq * 512:(tq + 1) * 512], start=st_, stop=sp),
                    deps=[wevs[c][1]] + war_k)
            eq = p.op("vector", lambda v, tq=tq, bq=bq: v.tensor_scalar(qf[:, tq * 512:(tq + 1) * 512], bq[:, 0:512], 1.0 / 64.0, None, ALU.mult), deps=[mq])
            ek = p.op("vector", lambda v, tq=tq, bk=bk: v.tensor_copy(kf[:, tq * 512:(tq + 1) * 512], bk[:, 0:512]), deps=[mk])
            last_bank_user[tq % 2] = eq
            last_bank_user[2 + tq % 2] = ek
            qk_done.append((eq, ek))
        for t in range(NT):
            bi = 4 + t % 2
            bank = cx.pb[bi]
            mv = None
            for c in range(NCH):
                st_, sp = (c == 0), (c == NCH - 1)
                war = [last_bank_user.get(bi)] if c == 0 else []
                mv = p.op("tensor", lambda tt, c=c, t=t, st_=st_, sp=sp, bank=bank: tt.matmul(
                    bank[:, 0:128], cx.xnf[:, c, t * 128:(t + 1) * 128], wqkv[:, c, 256:384], start=st_, stop=sp),
                    deps=[wevs[c][2]] + list(xnf_evs) + war)
            e1 = p.op("vector", lambda v, t=t, bank=bank: v.tensor_copy(vt[:, t, 0:64], bank[:, 0:64]), deps=[mv])
            e2 = p.op("vector", lambda v, t=t, bank=bank: v.tensor_copy(vt2[:, t, 0:64], bank[:, 64:128]), deps=[mv, e1])
            e3 = p.op("vector", lambda v, t=t: v.tensor_scalar(vt[:, t, 64:65], cx.ones_f[:, 0:1], 0.0, 1.0, ALU.mult, ALU.add), deps=[cx.ev_ones] + prev_use)
            e4 = p.op("vector", lambda v, t=t: v.tensor_scalar(vt2[:, t, 64:65], cx.ones_f[:, 0:1], 0.0, 1.0, ALU.mult, ALU.add), deps=[cx.ev_ones] + prev_use)
            last_bank_user[bi] = e2
            v_done.append((e1, e2, e3, e4))

        if "qkd" in DBG and hp == 0:
            dqf = cx.nc.dram_tensor("y_qf", [PT, T], BF16, kind="ExternalOutput").ap()
            dkf = cx.nc.dram_tensor("y_kf", [PT, T], BF16, kind="ExternalOutput").ap()
            cx.dbg_evs.append(p.dma(cx.ch_m, lambda e: e.dma_start(out=dqf, in_=qf[:, :]), deps=[ev for pair in qk_done for ev in pair]))
            cx.dbg_evs.append(p.dma(cx.ch_m, lambda e: e.dma_start(out=dkf, in_=kf[:, :]), deps=[ev for pair in qk_done for ev in pair]))
        nr = cx.small
        e_sq = p.op("scalar", lambda s: s.activation(sq_bf[:, 0:T], qf[:, :], AF.Square), deps=[e for e, _ in qk_done] + prev_use)
        nevs = []
        for part, e_src in ((0, e_sq), (1, None)):
            if part == 1:
                e_src = p.op("scalar", lambda s: s.activation(sq_bf[:, 0:T], kf[:, :], AF.Square), deps=[e for _, e in qk_done] + [e for e, _ in nevs])
            for hh in range(2):
                for tq in range(4):
                    bank = cx.pb[0]
                    war = [last_bank_user.get(0)] if not nevs else [nevs[-1][1]]
                    em = p.op("tensor", lambda tt, hh=hh, tq=tq, bank=bank: tt.matmul(
                        bank[0:1, 0:512], cx.blk_bf[:, hh:hh + 1],
                        sq_bf[:, tq * 512:(tq + 1) * 512], start=True, stop=True),
                        deps=[e_src, cx.ev_blk] + war)
                    er = p.op("vector", lambda v, part=part, hh=hh, tq=tq, bank=bank: v.tensor_reduce(
                        nr[0:1, 8 + part * 8 + 4 * hh + tq:9 + part * 8 + 4 * hh + tq], bank[0:1, 0:512], axis=AX.X, op=ALU.max), deps=[em])
                    nevs.append((em, er))
        last_bank_user[0] = nevs[-1][1]
        dep_nr = [er for _, er in nevs]
        e_q2 = p.op("vector", lambda v: v.tensor_reduce(nr[0:1, 35:36], nr[0:1, 8:16], axis=AX.X, op=ALU.max), deps=dep_nr)
        e_k2 = p.op("vector", lambda v: v.tensor_reduce(nr[0:1, 36:37], nr[0:1, 16:24], axis=AX.X, op=ALU.max), deps=dep_nr)
        e_sq2 = p.op("scalar", lambda s: s.activation(nr[0:1, 35:36], nr[0:1, 35:36], AF.Sqrt), deps=[e_q2])
        e_sk2 = p.op("scalar", lambda s: s.activation(nr[0:1, 36:37], nr[0:1, 36:37], AF.Sqrt), deps=[e_k2])
        e_c1 = p.op("vector", lambda v: v.tensor_tensor(nr[0:1, 37:38], nr[0:1, 35:36], nr[0:1, 36:37], ALU.mult), deps=[e_sq2, e_sk2])
        e_c2 = p.op("vector", lambda v: v.tensor_scalar(nr[0:1, 37:38], nr[0:1, 37:38], 1.02, None, ALU.mult), deps=[e_c1])
        e_c3 = p.op("vector", lambda v: v.tensor_scalar(nr[0:1, 38:39], nr[0:1, 34:35], 1.0 / 64.0, None, ALU.mult), deps=[e_r2])
        e_c4 = p.op("vector", lambda v: v.tensor_tensor(nr[0:1, 39:40], nr[0:1, 37:38], nr[0:1, 38:39], ALU.add), deps=[e_c2, e_c3])
        e_c5 = p.op("vector", lambda v: v.tensor_scalar(nr[0:1, 39:40], nr[0:1, 39:40], -1.0, None, ALU.mult), deps=[e_c4])
        nc_bf = A3[0:1, 2049:2050]
        e_c6 = p.op("vector", lambda v: v.tensor_copy(nc_bf, nr[0:1, 39:40]), deps=[e_c5])
        e_c7 = p.op("tensor", lambda tt: tt.transpose(
            cx.pbt[1][:, 0:1], nc_bf.broadcast_to((1, 128)), cx.eye_bf[0:1, 0:1]),
            deps=[e_c6, cx.ev_eye, last_bank_user.get(("t", 1))])
        e_negc = p.op("vector", lambda v: v.tensor_copy(negc, cx.pbt[1][:, 0:1]), deps=[e_c7])
        last_bank_user[("t", 1)] = e_negc

        if "attn" not in STAGES:
            prev_use = [qk_done[-1][0], qk_done[-1][1], v_done[-1][1]]
            cx._l0_otm_evs = prev_use
            continue
        exp_by_buf = {}
        pv_last_for_buf = {}
        po_banks = (cx.pb[4], cx.pb[5])
        po_war = [last_bank_user.get(4), last_bank_user.get(5)]
        for head in range(2):
            kbase = 64 * head
            hgl = 2 * hp + head
            po = po_banks[head]
            vtile = vt if head == 0 else vt2
            for qg in range(4):
                n_kt = 4 * (qg + 1)
                pv_evs = []
                for kt in range(n_kt):
                    bi = 2 * head + (kt % 2)
                    lb = cx.pb[bi][:, 0:512]
                    pbuf = pP[2 * head + (kt % 2)]
                    war_l = [exp_by_buf.get(bi), last_bank_user.get(bi)]
                    m1 = p.op("tensor", lambda tt, kt=kt, qg=qg, lb=lb, kbase=kbase: tt.matmul(
                        lb, kf[kbase:kbase + 64, kt * 128:(kt + 1) * 128],
                        qf[kbase:kbase + 64, qg * 512:(qg + 1) * 512], start=True, stop=False),
                        deps=[qk_done[-1][0], qk_done[-1][1]] + war_l)
                    soff = qg * 512 - kt * 128 + 384
                    m2 = p.op("tensor", lambda tt, lb=lb, soff=soff, head=head: tt.matmul(
                        lb, cx.J_bf[:, :], strips[:, head, soff:soff + 512], start=False, stop=True),
                        deps=[strip_ev[head], cx.ev_J, m1])
                    war_p = [pv_last_for_buf.get(2 * head + (kt % 2))]
                    e_exp = p.op("scalar", lambda s, lb=lb, pbuf=pbuf: s.activation(
                        pbuf, lb, AF.Exp, bias=negc[:, 0:1]), deps=[m2, e_negc] + war_p)
                    exp_by_buf[bi] = e_exp
                    if "ptile" in DBG and hp == 0 and head == 0 and qg == 0 and kt == 0:
                        dl = cx.nc.dram_tensor("y_logit", [PT, 512], F32, kind="ExternalOutput").ap()
                        dpp = cx.nc.dram_tensor("y_p", [PT, 512], BF16, kind="ExternalOutput").ap()
                        e_cp1 = p.op("vector", lambda v, lb=lb: v.tensor_copy(cx.sD[:, 0:512] if False else cx.sB[:, 0:512], lb), deps=[e_exp])
                        cx.dbg_evs.append(p.dma(cx.ch_m, lambda e: e.dma_start(out=dl, in_=cx.sB[:, 0:512]), deps=[e_cp1]))
                        cx.dbg_evs.append(p.dma(cx.ch_m, lambda e, pbuf=pbuf: e.dma_start(out=dpp, in_=pbuf), deps=[e_exp]))
                    for sub in range(4):
                        st_ = (kt == 0 and sub == 0)
                        sp = (kt == n_kt - 1)
                        dep_po = [last_bank_user.get(4 + head)] if kt == 0 else []
                        mm = p.op("tensor", lambda tt, pbuf=pbuf, sub=sub, kt=kt, vtile=vtile, po=po, st_=st_, sp=sp: tt.matmul(
                            po[:, sub * 68:sub * 68 + 65], pbuf[:, sub * 128:(sub + 1) * 128],
                            vtile[:, kt, 0:65], start=st_, stop=sp),
                            deps=[e_exp, v_done[kt][0], v_done[kt][1], v_done[kt][2], v_done[kt][3]] + dep_po)
                        pv_evs.append(mm)
                    pv_last_for_buf[2 * head + (kt % 2)] = pv_evs[-1]
                if "po" in DBG and hp == 0 and head == 0 and qg == 0:
                    dpo = cx.nc.dram_tensor("y_po", [PT, 512], F32, kind="ExternalOutput").ap()
                    e_cpo = p.op("vector", lambda v, po=po: v.tensor_copy(cx.sB[:, 0:512], po[:, 0:512]), deps=[pv_evs[-1]])
                    cx.dbg_evs.append(p.dma(cx.ch_m, lambda e: e.dma_start(out=dpo, in_=cx.sB[:, 0:512]), deps=[e_cpo]))
                e_prev = None
                for sub in range(4):
                    tq_idx = qg * 4 + sub
                    dn = cx.small[:, 42 + head * 2:43 + head * 2]
                    e_g = p.op("vector", lambda v, po=po, sub=sub, dn=dn: v.tensor_scalar(
                        dn, po[:, sub * 68 + 64:sub * 68 + 65], 1e-38, None, ALU.max), deps=pv_evs[-4:] + ([e_prev] if e_prev else []))
                    e_rc = p.op("vector", lambda v, dn=dn: v.reciprocal(dn, dn), deps=[e_g])
                    e_o = p.op("vector", lambda v, po=po, sub=sub, tq_idx=tq_idx, hgl=hgl, dn=dn: v.tensor_scalar(
                        o_tm[:, tq_idx, hgl * 64:(hgl + 1) * 64], po[:, sub * 68:sub * 68 + 64], dn[:, 0:1], None, ALU.mult), deps=[e_rc])
                    e_prev = e_o
                if head == 0:
                    last_bank_user[4] = e_prev
                else:
                    last_bank_user[5] = e_prev
                po_war = [last_bank_user.get(4), last_bank_user.get(5)]
        for bi in range(4):
            if exp_by_buf.get(bi) is not None:
                last_bank_user[bi] = exp_by_buf[bi]
        prev_use = [last_bank_user[4], last_bank_user[5], exp_by_buf.get(0), exp_by_buf.get(1), exp_by_buf.get(2), exp_by_buf.get(3)]
        cx._l0_otm_evs = prev_use

    o_done = prev_use
    if "x0" in DBG:
        dx0 = cx.nc.dram_tensor("y_x0", [PT, NT * D], F32, kind="ExternalOutput").ap()
        cx.dbg_evs.append(p.dma(cx.ch_m, lambda e: e.dma_start(out=dx0, in_=cx.x_sb[:, :, :].rearrange("p t w -> p (t w)")), deps=[cx.ev_x]))
    if "stats" in DBG:
        ds = cx.nc.dram_tensor("y_stats", [PT, 2 * NT], F32, kind="ExternalOutput").ap()
        cx.dbg_evs.append(p.dma(cx.ch_m, lambda e: e.dma_start(out=ds, in_=cx.ln_stats[:, :]), deps=list(stat_evs)))
    if "xnf" in DBG:
        dx = cx.nc.dram_tensor("y_xnf", [PT, NCH * T], BF16, kind="ExternalOutput").ap()
        cx.dbg_evs.append(p.dma(cx.ch_m, lambda e: e.dma_start(out=dx, in_=cx.xnf[:, :, :].rearrange("p c t -> p (c t)")), deps=list(xnf_evs)))
    if "otm" in DBG:
        do = cx.nc.dram_tensor("y_otm", [PT, NT * D], BF16, kind="ExternalOutput").ap()
        cx.dbg_evs.append(p.dma(cx.ch_m, lambda e: e.dma_start(out=do, in_=o_tm[:, :, :].rearrange("p t w -> p (t w)")), deps=o_done))

    xq_evs = []
    war = list(o_done)
    for t in (range(NT) if ("ffn" in STAGES or "xq" in STAGES) else []):
        xn = cx.sA[:, 0:D]
        e_xn = p.op("vector", lambda v, t=t: v.tensor_scalar(
            xn, cx.x_sb[:, t, :], cx.ln_stats[:, 2 * t:2 * t + 1], cx.ln_stats[:, 2 * t + 1:2 * t + 2],
            ALU.mult, ALU.add), deps=[stat_evs[t]] + war)
        xq_bf = A3[:, 0:1024]
        ab_ap = A3[:, 1024:3072].bitcast(F32)
        m_ap = A3[:, 3072:3584].bitcast(U8)[:, 0:1024]
        e_q = emit_fp4(cx, xq_bf, xn, 64, [e_xn], ab_ap, m_ap)
        tevs = []
        for c in range(NCH):
            bank = cx.pbt[c % 2]
            war_b = [last_bank_user.get(("t", c % 2))] if c < 2 else [tevs[-2][1]]
            e_t = p.op("tensor", lambda tt, c=c, bank=bank, xq_bf=xq_bf: tt.transpose(
                bank[:, 0:128], xq_bf[:, c * 128:(c + 1) * 128], cx.eye_bf[:, :]),
                deps=[e_q, cx.ev_eye] + war_b)
            e_c = p.op("vector", lambda v, t=t, c=c, bank=bank: v.tensor_copy(
                cx.xnf[:, c, t * 128:(t + 1) * 128], bank[:, 0:128]), deps=[e_t])
            tevs.append((e_t, e_c))
        last_bank_user[("t", 0)] = tevs[-2][1]
        last_bank_user[("t", 1)] = tevs[-1][1]
        xq_evs.append(tevs[-1][1])
        war = [e_q, tevs[-1][1], tevs[-2][1]]
    xq_all = list(xq_evs)

    if "bline" in DBG:
        db = cx.nc.dram_tensor("y_bline", [16, 4096], F32, kind="ExternalOutput").ap()
        cx.dbg_evs.append(p.dma(cx.ch_m, lambda e: e.dma_start(out=db, in_=cx.scr_dram[0:16, 0:4096]), deps=[ev_bl]))
    if "small" in DBG:
        dsm = cx.nc.dram_tensor("y_small", [PT, 64], F32, kind="ExternalOutput").ap()
        cx.dbg_evs.append(p.dma(cx.ch_m, lambda e: e.dma_start(out=dsm, in_=cx.small[:, :]), deps=o_done))
    if "xq" in DBG:
        dq = cx.nc.dram_tensor("y_xq", [PT, NCH * T], BF16, kind="ExternalOutput").ap()
        cx.dbg_evs.append(p.dma(cx.ch_m, lambda e: e.dma_start(out=dq, in_=cx.xnf[:, :, :].rearrange("p c t -> p (c t)")), deps=list(xq_all)))
    wo_sb = A3[:, 4608:12800].rearrange("p (c w) -> p c w", c=NCH)
    wo_evs = [p.dma(cx.ch_wg, lambda e, c=c: e.dma_start(
        out=wo_sb[:, c, :], in_=d["l0_wo"][c * 128:(c + 1) * 128, :]), deps=cx._l0_otm_evs if False else []) for c in range(NCH)]
    ones_row = A3[:, 0:128]
    bo_bf = A3[0:1, 384:1408]
    ev_bo = p.dma(cx.ch_wg, lambda e: e.dma_start(out=bo_bf, in_=d["l0_bo"]), deps=(xq_all[-1:] if xq_all else o_done))
    e_or = p.op("vector", lambda v: v.tensor_scalar(ones_row[0:1, 0:128], cx.ones_f[0:1, 0:1].broadcast_to((1, 128)), 1.0, None, ALU.mult), deps=[cx.ev_ones])

    stg = [A3[:, 128 + i * 128:128 + (i + 1) * 128] for i in range(2)]
    x_upd = []
    for qt in (range(NT) if "proj" in STAGES else []):
        b_out = (cx.pb[4], cx.pb[5])
        mm_last = [None, None]
        for c in range(NCH):
            bank = cx.pbt[c % 2]
            war_b = [last_bank_user.get(("t", c % 2))]
            e_t = p.op("tensor", lambda tt, qt=qt, c=c, bank=bank: tt.transpose(
                bank[:, 0:128], o_tm[:, qt, c * 128:(c + 1) * 128], cx.eye_bf[:, :]),
                deps=o_done + war_b + [cx.ev_eye])
            e_s = p.op("vector", lambda v, c=c, bank=bank: v.tensor_copy(stg[c % 2][:, 0:128], bank[:, 0:128]),
                       deps=[e_t] + ([mm_last[0], mm_last[1]] if c >= 2 else [last_bank_user.get(("stg", c % 2))]))
            last_bank_user[("t", c % 2)] = e_s
            for half in range(2):
                st_, sp = (c == 0), (c == NCH - 1)
                war_o = [last_bank_user.get(4 + half)] if c == 0 else []
                mm = p.op("tensor", lambda tt, c=c, half=half, qt=qt, st_=st_, sp=sp: tt.matmul(
                    b_out[half][:, 0:512], stg[c % 2][:, 0:128], wo_sb[:, c, half * 512:(half + 1) * 512],
                    start=st_, stop=False), deps=[e_s, wo_evs[c]] + war_o)
                mm_last[half] = mm
            last_bank_user[("stg", c % 2)] = mm_last[0] if c % 2 == 0 else mm_last[1]
        for half in range(2):
            mm = p.op("tensor", lambda tt, half=half: tt.matmul(
                b_out[half][:, 0:512], ones_row[0:1, 0:128], bo_bf[0:1, half * 512:(half + 1) * 512],
                start=False, stop=True), deps=[mm_last[half], e_or, ev_bo])
            mm_last[half] = mm
        for half in range(2):
            e_x = p.op("vector", lambda v, qt=qt, half=half: v.tensor_tensor(
                cx.x_sb[:, qt, half * 512:(half + 1) * 512], cx.x_sb[:, qt, half * 512:(half + 1) * 512],
                b_out[half][:, 0:512], ALU.add), deps=[mm_last[half]])
            last_bank_user[4 + half] = e_x
            x_upd.append(e_x)

    HH = 1408
    wgq = A1[:, 0:11264].rearrange("p (c w) -> p c w", c=NCH)
    wuq = A2[:, 0:11264].rearrange("p (c w) -> p c w", c=NCH)
    wdq = A3[:, 0:11264].rearrange("p (k w) -> p k w", k=11)
    h_bf = A1[:, 11264:12672]
    hstg = [A2[:, 11264 + i * 128:11264 + (i + 1) * 128] for i in range(2)]

    ffn_x_upd = []
    war_w = list(x_upd) + xq_all
    for hh in (range(2) if "ffn" in STAGES else []):
        wq_evs = {"g": [], "u": [], "d": []}
        for (key, src, ntile, rowoff, wdst) in (
                ("g", d["l0_wg"], 11, hh * HH, wgq), ("u", d["l0_wu"], 11, hh * HH, wuq)):
            for k in range(ntile):
                e_l = p.dma(cx.ch_wg, lambda e, src=src, k=k, rowoff=rowoff: e.dma_start(
                    out=cx.sA[:, 0:1024].bitcast(BF16)[:, 0:1024] if False else cx.sA[:, 0:1024],
                    in_=src[rowoff + k * 128:rowoff + (k + 1) * 128, :]), deps=war_w)
                e_q = emit_fp4(cx, h_bf[:, 0:1024], cx.sA[:, 0:1024], 64, [e_l],
                               cx.AR1[:, 12672:15488].bitcast(F32)[:, 0:1024], cx.AR2[:, 11520:12032].bitcast(U8)[:, 0:1024])
                tevs = []
                for c in range(NCH):
                    bank = cx.pbt[c % 2]
                    war_b = [last_bank_user.get(("t", c % 2))] if c < 2 else [tevs[-2][1]]
                    e_t = p.op("tensor", lambda tt, c=c, bank=bank: tt.transpose(
                        bank[:, 0:128], h_bf[:, c * 128:(c + 1) * 128], cx.eye_bf[:, :]),
                        deps=[e_q, cx.ev_eye] + war_b)
                    e_c = p.op("vector", lambda v, c=c, k=k, bank=bank, wdst=wdst: v.tensor_copy(
                        wdst[:, c, k * 128:(k + 1) * 128], bank[:, 0:128]), deps=[e_t])
                    tevs.append((e_t, e_c))
                last_bank_user[("t", 0)] = tevs[-2][1]
                last_bank_user[("t", 1)] = tevs[-1][1]
                war_w = [e_q, tevs[-1][1]]
                wq_evs[key].append(tevs[-1][1])
        for k8 in range(NCH):
            e_l = p.dma(cx.ch_wg, lambda e, k8=k8, hh=hh: e.dma_start(
                out=cx.sA[:, 0:HH], in_=d["l0_wd"][k8 * 128:(k8 + 1) * 128, hh * HH:(hh + 1) * HH]), deps=war_w)
            e_q = emit_fp4(cx, h_bf[:, 0:HH], cx.sA[:, 0:HH], 88, [e_l],
                           cx.AR1[:, 12672:15488].bitcast(F32), cx.AR2[:, 11520:12224].bitcast(U8))
            tevs = []
            for k in range(11):
                bank = cx.pbt[k % 2]
                war_b = [last_bank_user.get(("t", k % 2))] if k < 2 else [tevs[-2][1]]
                e_t = p.op("tensor", lambda tt, k=k, bank=bank: tt.transpose(
                    bank[:, 0:128], h_bf[:, k * 128:(k + 1) * 128], cx.eye_bf[:, :]),
                    deps=[e_q, cx.ev_eye] + war_b)
                e_c = p.op("vector", lambda v, k=k, k8=k8, bank=bank: v.tensor_copy(
                    wdq[:, k, k8 * 128:(k8 + 1) * 128], bank[:, 0:128]), deps=[e_t])
                tevs.append((e_t, e_c))
            last_bank_user[("t", 0)] = tevs[-2][1] if len(tevs) >= 2 else tevs[-1][1]
            last_bank_user[("t", 1)] = tevs[-1][1]
            war_w = [e_q, tevs[-1][1]]
            wq_evs["d"].append(tevs[-1][1])
        dep_w = wq_evs["g"] + wq_evs["u"] + wq_evs["d"]

        for t in range(NT):
            mg_last = [None] * 3
            mu_last = [None] * 3
            for c in range(NCH):
                st_, sp = (c == 0), (c == NCH - 1)
                for ni in range(3):
                    nw = 512 if ni < 2 else 384
                    war_g = [last_bank_user.get(ni)] if c == 0 else []
                    mg_last[ni] = p.op("tensor", lambda tt, c=c, t=t, ni=ni, nw=nw, st_=st_, sp=sp: tt.matmul(
                        cx.pb[ni][:, 0:nw], cx.xnf[:, c, t * 128:(t + 1) * 128],
                        wgq[:, c, ni * 512:ni * 512 + nw], start=st_, stop=sp),
                        deps=[e for e in ([wq_evs["g"][-1]] + xq_all)] + war_g)
                    war_u = [last_bank_user.get(3 + ni)] if c == 0 else []
                    mu_last[ni] = p.op("tensor", lambda tt, c=c, t=t, ni=ni, nw=nw, st_=st_, sp=sp: tt.matmul(
                        cx.pb[3 + ni][:, 0:nw], cx.xnf[:, c, t * 128:(t + 1) * 128],
                        wuq[:, c, ni * 512:ni * 512 + nw], start=st_, stop=sp),
                        deps=[wq_evs["u"][-1]] + war_u)
            e_sl = []
            for ni in range(3):
                nw = 512 if ni < 2 else 384
                e_s = p.op("scalar", lambda s, ni=ni, nw=nw: s.activation(
                    cx.sC[:, ni * 512:ni * 512 + nw], cx.pb[ni][:, 0:nw], AF.Silu), deps=[mg_last[ni]])
                e_m = p.op("vector", lambda v, ni=ni, nw=nw: v.tensor_tensor(
                    cx.sA[:, ni * 512:ni * 512 + nw], cx.sC[:, ni * 512:ni * 512 + nw],
                    cx.pb[3 + ni][:, 0:nw], ALU.mult), deps=[e_s, mu_last[ni]])
                e_sl.append(e_m)
                last_bank_user[ni] = e_m
                last_bank_user[3 + ni] = e_m
            e_q = emit_fp4(cx, h_bf[:, 0:HH], cx.sA[:, 0:HH], 88, e_sl,
                           cx.AR1[:, 12672:15488].bitcast(F32), cx.AR2[:, 11520:12224].bitcast(U8))
            md_last = [None, None]
            for k in range(11):
                bank = cx.pbt[k % 2]
                war_b = [last_bank_user.get(("t", k % 2))] if k < 2 else [md_last[0], md_last[1]]
                e_t = p.op("tensor", lambda tt, k=k, bank=bank: tt.transpose(
                    bank[:, 0:128], h_bf[:, k * 128:(k + 1) * 128], cx.eye_bf[:, :]),
                    deps=[e_q] + war_b + [cx.ev_eye])
                e_s = p.op("vector", lambda v, k=k, bank=bank: v.tensor_copy(hstg[k % 2][:, 0:128], bank[:, 0:128]),
                           deps=[e_t] + ([md_last[0], md_last[1]] if k >= 2 else [last_bank_user.get(("hstg", k % 2))]))
                last_bank_user[("t", k % 2)] = e_s
                for half in range(2):
                    st_, sp = (k == 0), False
                    war_o = [last_bank_user.get(4 + half)] if k == 0 else []
                    md_last[half] = p.op("tensor", lambda tt, k=k, half=half, st_=st_: tt.matmul(
                        cx.pb[4 + half][:, 0:512], hstg[k % 2][:, 0:128],
                        wdq[:, k, half * 512:(half + 1) * 512], start=st_, stop=(k == 10)),
                        deps=[e_s, wq_evs["d"][-1]] + war_o)
                last_bank_user[("hstg", k % 2)] = md_last[1]
            for half in range(2):
                e_x = p.op("vector", lambda v, t=t, half=half: v.tensor_tensor(
                    cx.x_sb[:, t, half * 512:(half + 1) * 512], cx.x_sb[:, t, half * 512:(half + 1) * 512],
                    cx.pb[4 + half][:, 0:512], ALU.add), deps=[md_last[half]] + ([e for e in ffn_x_upd[-1:]] if ffn_x_upd else []))
                last_bank_user[4 + half] = e_x
                ffn_x_upd.append(e_x)
            war_w = [e_q]
    if "ffn" not in STAGES:
        return (x_upd if x_upd else xq_all)
    return ffn_x_upd[-2 * NT:]


_PROG_CACHE = {}



def emit_layer1(cx, stat_evs, xnf_evs):
    p, d = cx.p, cx.dram
    A1, A2, A3 = cx.AR1, cx.AR2, cx.AR3
    o_tm = A1[:, 0:16384].rearrange("p (t w) -> p t w", t=NT)
    wqkv = A2[:, 0:3072].rearrange("p (c w) -> p c w", c=NCH)
    qf = A2[:, 3072:5120]
    kf = A2[:, 5120:7168]
    vt = A2[:, 7168:8256].rearrange("p (t w) -> p t w", t=NT)
    vt2 = A2[:, 8256:9344].rearrange("p (t w) -> p t w", t=NT)
    stripw = A2[:, 9344:11392]
    glw_v = A2[:, 12228:12244].rearrange("p (c w) -> p c w", c=NCH)
    gvt = A2[:, 12800:13320]
    GKf = A2[:, 13320:13576].rearrange("p (c w) -> p c w", c=4)
    GVf = A2[:, 13576:13832].rearrange("p (c w) -> p c w", c=4)
    rw_sl = A2[:, 14344:14856].rearrange("p (c w) -> p c w", c=4)
    pP = [A3[:, i * 512:(i + 1) * 512] for i in range(4)]
    sq_bf = A3[:, 2048:4096]
    GQ = A3[:, 4096:12288].rearrange("p (c w) -> p c w", c=4)
    rem128 = A3[:, 4096:12288].rearrange("p (c w) -> p c w", c=4)
    pooled = A3[:, 12288:12800]
    pool3 = pooled.rearrange("p (c l) -> p c l", c=NCH)
    negc = cx.small[:, 40:41]
    negcl = cx.small[:, 41:42]
    glbuf = cx.sC[:, 1024:1088]
    nr = cx.small
    bw = cx.bw
    prev_use = list(stat_evs)

    e_mx = None
    for c in range(NCH):
        e_mx = p.op("vector", lambda v, c=c, e_mx=e_mx: v.tensor_reduce(
            nr[:, 48 + (1 if e_mx else 0):49 + (1 if e_mx else 0)], cx.xnf[:, c, :],
            axis=AX.X, op=ALU.max, apply_absolute_value=True), deps=list(xnf_evs) + ([e_mx] if e_mx else []))
        if c:
            e_mx = p.op("vector", lambda v: v.tensor_tensor(nr[:, 48:49], nr[:, 48:49], nr[:, 49:50], ALU.max), deps=[e_mx])
    e_xb = p.op("vector", lambda v: v.tensor_copy(A3[:, 2050:2051], nr[:, 48:49]), deps=[e_mx])
    e_xt = p.op("tensor", lambda tt: tt.transpose(
        cx.pbt[1][:, 0:128], A3[:, 2050:2051].broadcast_to((PT, 128)), cx.eye_bf[:, :]),
        deps=[e_xb, cx.ev_eye, bw.get(("t", 1))])
    e_xr = p.op("vector", lambda v: v.tensor_reduce(nr[0:1, 50:51], cx.pbt[1][0:1, 0:128], axis=AX.X, op=ALU.max), deps=[e_xt])
    bw[("t", 1)] = e_xr
    ev_ci1 = p.dma(cx.ch_m, lambda e: e.dma_start(out=d["cc_i1"], in_=nr[0:1, 50:52]), deps=[e_xr])
    ev_cc1 = p.op("gpsimd", lambda g: g.collective_compute(
        "AllReduce", ALU.max, replica_groups=[list(range(NCORES))],
        ins=[cx.cc_i[0].ap().opt()], outs=[cx.cc_o[0].ap().opt()]), deps=[ev_ci1])
    ev_co1 = p.dma(cx.ch_m, lambda e: e.dma_start(out=nr[0:1, 50:52], in_=d["cc_o1"]), deps=[ev_cc1])

    pool_evs = []
    for c in range(NCH):
        e_r = p.op("vector", lambda v, c=c: v.tensor_reduce(
            cx.sA[:, 0:64], cx.xnf[:, c, :].rearrange("p (l b) -> p l b", b=32), axis=AX.X, op=ALU.add),
            deps=list(xnf_evs) + prev_use + pool_evs[-1:])
        e_s = p.op("vector", lambda v, c=c: v.tensor_scalar(pool3[:, c, :], cx.sA[:, 0:64], 1.0 / 32.0, None, ALU.mult), deps=[e_r])
        pool_evs.append(e_s)
    wg_sl = A2[:, 0:1024].rearrange("p (c w) -> p c w", c=NCH)
    gq_evs = []
    for oc in range(4):
        wevq = [p.dma(cx.ch_wg, lambda e, c=c, oc=oc: e.dma_start(
            out=wg_sl[:, c, :], in_=d["l1_wgq"][c * 128:(c + 1) * 128, oc * 128:(oc + 1) * 128]),
            deps=prev_use + gq_evs[-1:]) for c in range(NCH)]
        for tq in range(4):
            bank = cx.pb[tq % 2]
            mm = None
            for c in range(NCH):
                st_, sp = (c == 0), (c == NCH - 1)
                war = [bw.get(tq % 2)] if c == 0 else []
                mm = p.op("tensor", lambda tt, c=c, tq=tq, st_=st_, sp=sp, bank=bank: tt.matmul(
                    bank[:, 0:512], wg_sl[:, c, :], cx.xnf[:, c, tq * 512:(tq + 1) * 512], start=st_, stop=sp),
                    deps=[wevq[c]] + list(xnf_evs) + war)
            ev = p.op("vector", lambda v, oc=oc, tq=tq, bank=bank: v.tensor_scalar(
                GQ[:, oc, tq * 512:(tq + 1) * 512], bank[:, 0:512], 1.0 / 8.0, None, ALU.mult), deps=[mm])
            bw[tq % 2] = ev
            gq_evs.append(ev)
    gkv_evs = []
    for (wn, dst) in (("l1_wgk", GKf), ("l1_wgv", GVf)):
        for oc in range(4):
            wev = [p.dma(cx.ch_wg, lambda e, c=c, oc=oc, wn=wn: e.dma_start(
                out=wg_sl[:, c, :], in_=d[wn][c * 128:(c + 1) * 128, oc * 128:(oc + 1) * 128]),
                deps=gq_evs[-1:] + gkv_evs[-1:]) for c in range(NCH)]
            bank = cx.pb[2]
            mm = None
            for c in range(NCH):
                st_, sp = (c == 0), (c == NCH - 1)
                war = [bw.get(2)] if c == 0 else []
                mm = p.op("tensor", lambda tt, c=c, st_=st_, sp=sp, bank=bank: tt.matmul(
                    bank[:, 0:64], wg_sl[:, c, :], pool3[:, c, :], start=st_, stop=sp),
                    deps=[wev[c]] + pool_evs + war)
            ev = p.op("vector", lambda v, oc=oc, dst=dst, bank=bank: v.tensor_copy(dst[:, oc, :], bank[:, 0:64]), deps=[mm])
            bw[2] = ev
            gkv_evs.append(ev)
    gvt_evs = []
    for c2 in range(4):
        e_t = p.op("tensor", lambda tt, c2=c2: tt.transpose(
            cx.pbt[0][0:64, 0:128], GVf[:, c2, :], cx.eye_bf[:, :]),
            deps=gkv_evs[-1:] + [cx.ev_eye, bw.get(("t", 0))])
        e_cs = []
        for half in range(2):
            lh = 2 * c2 + half
            e_c = p.op("vector", lambda v, lh=lh, half=half: v.tensor_copy(
                gvt[0:64, lh * 65:lh * 65 + 64], cx.pbt[0][0:64, half * 64:half * 64 + 64]), deps=[e_t] + e_cs[-1:])
            e_o = p.op("vector", lambda v, lh=lh: v.tensor_scalar(
                gvt[0:64, lh * 65 + 64:lh * 65 + 65], cx.ones_f[0:64, 0:1], 0.0, 1.0, ALU.mult, ALU.add), deps=[e_c, cx.ev_ones])
            e_cs.append(e_o)
            gvt_evs.append(e_o)
        bw[("t", 0)] = e_cs[-1]

    nevsl = []
    for c4 in range(4):
        e_sqg = p.op("scalar", lambda s, c4=c4: s.activation(sq_bf[:, 0:T], GQ[:, c4, :], AF.Square),
                     deps=gq_evs[-1:] + ([nevsl[-1][1]] if nevsl else []))
        for hh in range(2):
            for tq in range(4):
                bank = cx.pb[3]
                war = [bw.get(3)] if not nevsl else [nevsl[-1][1]]
                em = p.op("tensor", lambda tt, hh=hh, tq=tq, bank=bank: tt.matmul(
                    bank[0:1, 0:512], cx.blk_bf[:, hh:hh + 1], sq_bf[:, tq * 512:(tq + 1) * 512], start=True, stop=True),
                    deps=[e_sqg, cx.ev_blk] + war)
                er = p.op("vector", lambda v, bank=bank: v.tensor_reduce(
                    nr[0:1, 34:35], bank[0:1, 0:512], axis=AX.X, op=ALU.max), deps=[em])
                if nevsl:
                    er = p.op("vector", lambda v: v.tensor_tensor(nr[0:1, 33:34], nr[0:1, 33:34], nr[0:1, 34:35], ALU.max), deps=[er])
                else:
                    er = p.op("vector", lambda v: v.tensor_scalar(nr[0:1, 33:34], nr[0:1, 34:35], 1.0, None, ALU.mult), deps=[er])
                nevsl.append((em, er))
    nevk = []
    for c4 in range(4):
        e_sk = p.op("scalar", lambda s, c4=c4: s.activation(sq_bf[:, 0:64], GKf[:, c4, :], AF.Square),
                    deps=gkv_evs[-1:] + ([nevk[-1][1]] if nevk else [nevsl[-1][1]]))
        for hh in range(2):
            bank = cx.pb[3]
            war = [nevsl[-1][1]] if not nevk else [nevk[-1][1]]
            em = p.op("tensor", lambda tt, hh=hh, bank=bank: tt.matmul(
                bank[0:1, 0:64], cx.blk_bf[:, hh:hh + 1], sq_bf[:, 0:64], start=True, stop=True),
                deps=[e_sk, cx.ev_blk] + war)
            er = p.op("vector", lambda v, bank=bank: v.tensor_reduce(
                nr[0:1, 35:36], bank[0:1, 0:64], axis=AX.X, op=ALU.max), deps=[em])
            if nevk:
                er = p.op("vector", lambda v: v.tensor_tensor(nr[0:1, 36:37], nr[0:1, 36:37], nr[0:1, 35:36], ALU.max), deps=[er])
            else:
                er = p.op("vector", lambda v: v.tensor_scalar(nr[0:1, 36:37], nr[0:1, 35:36], 1.0, None, ALU.mult), deps=[er])
            nevk.append((em, er))
    bw[3] = nevk[-1][1]
    e_c1 = p.op("scalar", lambda s: s.activation(nr[0:1, 33:34], nr[0:1, 33:34], AF.Sqrt), deps=[nevsl[-1][1], nevk[-1][1]])
    e_c2 = p.op("scalar", lambda s: s.activation(nr[0:1, 36:37], nr[0:1, 36:37], AF.Sqrt), deps=[e_c1])
    e_c3 = p.op("vector", lambda v: v.tensor_tensor(nr[0:1, 37:38], nr[0:1, 33:34], nr[0:1, 36:37], ALU.mult), deps=[e_c2])
    e_c4 = p.op("vector", lambda v: v.tensor_scalar(nr[0:1, 37:38], nr[0:1, 37:38], -1.02, None, ALU.mult), deps=[e_c3])
    ncl_bf = A3[0:1, 2049:2050]
    e_c5 = p.op("vector", lambda v: v.tensor_copy(ncl_bf, nr[0:1, 37:38]), deps=[e_c4])
    e_c6 = p.op("tensor", lambda tt: tt.transpose(
        cx.pbt[1][:, 0:1], ncl_bf.broadcast_to((1, 128)), cx.eye_bf[0:1, 0:1]),
        deps=[e_c5, cx.ev_eye, bw.get(("t", 1))])
    e_negcl = p.op("vector", lambda v: v.tensor_copy(negcl, cx.pbt[1][:, 0:1]), deps=[e_c6])
    bw[("t", 1)] = e_negcl

    ev_stm = p.dma(cx.ch_wg, lambda e: e.dma_start(
        out=stripw[:, 0:2048], in_=bass.AP(d["c_mline"].tensor, 0, [[32, 128], [1, 2048]])), deps=prev_use)
    rem_evs = []
    stgr = pP[2]
    for c2 in range(4):
        for g in range(4):
            exp_h = []
            for half in range(2):
                kb = half * 64
                lb = cx.pb[2]
                m1 = p.op("tensor", lambda tt, g=g, c2=c2, kb=kb, lb=lb: tt.matmul(
                    lb[0:64, 0:512], GKf[kb:kb + 64, c2, :], GQ[kb:kb + 64, c2, g * 512:(g + 1) * 512], start=True, stop=False),
                    deps=gkv_evs[-1:] + gq_evs[-1:] + [bw.get(2)])
                m2 = p.op("tensor", lambda tt, g=g, lb=lb: tt.matmul(
                    lb[0:64, 0:512], cx.J_bf[:, 0:64], stripw[:, g * 512:(g + 1) * 512], start=False, stop=True),
                    deps=[ev_stm, cx.ev_J, m1])
                pbuf = pP[half]
                e_exp = p.op("scalar", lambda s, lb=lb, pbuf=pbuf: s.activation(
                    pbuf[0:64, 0:512], lb[0:64, 0:512], AF.Exp, bias=negcl[0:64, 0:1]),
                    deps=[m2, e_negcl, bw.get(("pl", half))])
                bw[2] = e_exp
                exp_h.append(e_exp)
                pv_l = []
                for sub in range(4):
                    po = cx.pb[4 + half]
                    mm = p.op("tensor", lambda tt, pbuf=pbuf, sub=sub, c2=c2, half=half, po=po: tt.matmul(
                        po[:, sub * 68:sub * 68 + 65], pbuf[0:64, sub * 128:(sub + 1) * 128],
                        gvt[0:64, (2 * c2 + half) * 65:(2 * c2 + half) * 65 + 65], start=(sub == 0), stop=True),
                        deps=[e_exp] + gvt_evs[-1:] + ([bw.get(4 + half)] if sub == 0 else []) + pv_l[-1:])
                    pv_l.append(mm)
                bw[("pl", half)] = pv_l[-1]
            for sub in range(4):
                tq_idx = g * 4 + sub
                eo_h = []
                for half in range(2):
                    po = cx.pb[4 + half]
                    dn = nr[:, 42 + half:43 + half]
                    e_g = p.op("vector", lambda v, po=po, sub=sub, dn=dn: v.tensor_scalar(
                        dn, po[:, sub * 68 + 64:sub * 68 + 65], 1e-38, None, ALU.max),
                        deps=[bw.get(("pl", half))] + eo_h[-1:] + ([bw.get("stgr")] if sub == 0 and half == 0 else []))
                    e_rc = p.op("vector", lambda v, dn=dn: v.reciprocal(dn, dn), deps=[e_g])
                    e_o = p.op("vector", lambda v, po=po, sub=sub, half=half, dn=dn: v.tensor_scalar(
                        stgr[:, half * 64:half * 64 + 64], po[:, sub * 68:sub * 68 + 64], dn[:, 0:1], None, ALU.mult),
                        deps=[e_rc])
                    eo_h.append(e_o)
                bw[4] = eo_h[0]
                bw[5] = eo_h[1]
                e_t = p.op("tensor", lambda tt: tt.transpose(
                    cx.pbt[0][:, 0:128], stgr[:, 0:128], cx.eye_bf[:, :]),
                    deps=eo_h + [cx.ev_eye, bw.get(("t", 0))])
                bw["stgr"] = e_t
                e_cp = p.op("vector", lambda v, c2=c2, tq_idx=tq_idx: v.tensor_copy(
                    rem128[:, c2, tq_idx * 128:(tq_idx + 1) * 128], cx.pbt[0][:, 0:128]), deps=[e_t])
                bw[("t", 0)] = e_cp
                rem_evs.append(e_cp)

    ev_stw = p.dma(cx.ch_wg, lambda e: e.dma_start(
        out=stripw[:, 0:1152], in_=bass.AP(d["c_wline"].tensor, 0, [[1, 128], [1, 1152]])),
        deps=rem_evs[-1:] + [bw[2]])
    ev_glw = p.dma(cx.ch_wg, lambda e: e.dma_start(
        out=glw_v[:, :, :], in_=bass.AP(d["l1_glw"].tensor, 0, [[2, 128], [256, 8], [1, 2]])), deps=prev_use)
    e_glw8 = p.op("vector", lambda v: v.tensor_scalar(A2[:, 12228:12244], A2[:, 12228:12244], 8.0, None, ALU.mult), deps=[ev_glw])
    vo_evs = []
    for t in range(NT):
        e3 = p.op("vector", lambda v, t=t: v.tensor_scalar(vt[:, t, 64:65], cx.ones_f[:, 0:1], 0.0, 1.0, ALU.mult, ALU.add), deps=[cx.ev_ones])
        e4 = p.op("vector", lambda v, t=t: v.tensor_scalar(vt2[:, t, 64:65], cx.ones_f[:, 0:1], 0.0, 1.0, ALU.mult, ALU.add), deps=[cx.ev_ones])
        vo_evs += [e3, e4]

    x_upd = []
    prev_hp = []
    last_xnf_reader = []
    for hp in range(NCH):
        wevs = []
        for c in range(NCH):
            e_q = p.dma(cx.ch_wg, lambda e, c=c, hp=hp: e.dma_start(
                out=wqkv[:, c, 0:128], in_=d["l1_wq"][c * 128:(c + 1) * 128, hp * 128:(hp + 1) * 128]),
                deps=prev_hp + gq_evs[-1:] + gkv_evs[-1:])
            e_k = p.dma(cx.ch_wg, lambda e, c=c, hp=hp: e.dma_start(
                out=wqkv[:, c, 128:256], in_=d["l1_wk"][c * 128:(c + 1) * 128, hp * 128:(hp + 1) * 128]),
                deps=prev_hp + gkv_evs[-1:])
            e_v = p.dma(cx.ch_wg, lambda e, c=c, hp=hp: e.dma_start(
                out=wqkv[:, c, 256:384], in_=d["l1_wv"][c * 128:(c + 1) * 128, hp * 128:(hp + 1) * 128]),
                deps=prev_hp + gkv_evs[-1:])
            wevs.append((e_q, e_k, e_v))
        e_rw = p.dma(cx.ch_wg, lambda e, hp=hp: e.dma_start(
            out=rw_sl[:, :, :], in_=bass.AP(d["l1_rout"].tensor, hp * 128, [[1024, 128], [131072, 4], [1, 128]])),
            deps=prev_hp + rem_evs[-1:])

        qk_done, v_done = [], []
        for tq in range(4):
            bq, bk = cx.pb[tq % 2], cx.pb[2 + tq % 2]
            mq = mk = None
            for c in range(NCH):
                st_, sp = (c == 0), (c == NCH - 1)
                war_q = [bw.get(tq % 2)] if c == 0 else []
                war_k = [bw.get(2 + tq % 2)] if c == 0 else []
                mq = p.op("tensor", lambda tt, c=c, tq=tq, st_=st_, sp=sp, bq=bq: tt.matmul(
                    bq[:, 0:512], wqkv[:, c, 0:128], cx.xnf[:, c, tq * 512:(tq + 1) * 512], start=st_, stop=sp),
                    deps=[wevs[c][0]] + list(xnf_evs) + war_q)
                mk = p.op("tensor", lambda tt, c=c, tq=tq, st_=st_, sp=sp, bk=bk: tt.matmul(
                    bk[:, 0:512], wqkv[:, c, 128:256], cx.xnf[:, c, tq * 512:(tq + 1) * 512], start=st_, stop=sp),
                    deps=[wevs[c][1]] + war_k)
            eq = p.op("vector", lambda v, tq=tq, bq=bq: v.tensor_scalar(qf[:, tq * 512:(tq + 1) * 512], bq[:, 0:512], 1.0 / 8.0, None, ALU.mult), deps=[mq])
            ek = p.op("vector", lambda v, tq=tq, bk=bk: v.tensor_copy(kf[:, tq * 512:(tq + 1) * 512], bk[:, 0:512]), deps=[mk])
            bw[tq % 2] = eq
            bw[2 + tq % 2] = ek
            qk_done.append((eq, ek))
        for t in range(NT):
            bi = 4 + t % 2
            bank = cx.pb[bi]
            mv = None
            for c in range(NCH):
                st_, sp = (c == 0), (c == NCH - 1)
                war = [bw.get(bi)] if c == 0 else []
                mv = p.op("tensor", lambda tt, c=c, t=t, st_=st_, sp=sp, bank=bank: tt.matmul(
                    bank[:, 0:128], cx.xnf[:, c, t * 128:(t + 1) * 128], wqkv[:, c, 256:384], start=st_, stop=sp),
                    deps=[wevs[c][2]] + list(xnf_evs) + war)
            e1 = p.op("vector", lambda v, t=t, bank=bank: v.tensor_copy(vt[:, t, 0:64], bank[:, 0:64]), deps=[mv] + vo_evs[-1:])
            e2 = p.op("vector", lambda v, t=t, bank=bank: v.tensor_copy(vt2[:, t, 0:64], bank[:, 64:128]), deps=[mv, e1])
            bw[bi] = e2
            v_done.append((e1, e2))
        if hp == NCH - 1:
            last_xnf_reader = [qk_done[-1][0], qk_done[-1][1], v_done[-1][1]]

        e_sq = p.op("scalar", lambda s: s.activation(sq_bf[:, 0:T], qf[:, :], AF.Square),
                    deps=[e for e, _ in qk_done] + prev_hp)
        nevs = []
        for part in range(2):
            if part == 1:
                e_src = p.op("scalar", lambda s: s.activation(sq_bf[:, 0:T], kf[:, :], AF.Square),
                             deps=[e for _, e in qk_done] + [nevs[-1][1]])
            else:
                e_src = e_sq
            for hh in range(2):
                for tq in range(4):
                    bank = cx.pb[3]
                    war = [bw.get(3)] if not nevs else [nevs[-1][1]]
                    em = p.op("tensor", lambda tt, hh=hh, tq=tq, bank=bank, e_src=e_src: tt.matmul(
                        bank[0:1, 0:512], cx.blk_bf[:, hh:hh + 1], sq_bf[:, tq * 512:(tq + 1) * 512], start=True, stop=True),
                        deps=[e_src, cx.ev_blk] + war)
                    er = p.op("vector", lambda v, part=part, bank=bank: v.tensor_reduce(
                        nr[0:1, 34 + part:35 + part], bank[0:1, 0:512], axis=AX.X, op=ALU.max), deps=[em])
                    if tq == 0 and hh == 0:
                        er = p.op("vector", lambda v, part=part: v.tensor_scalar(
                            nr[0:1, 36 + part:37 + part], nr[0:1, 34 + part:35 + part], 1.0, None, ALU.mult), deps=[er])
                    else:
                        er = p.op("vector", lambda v, part=part: v.tensor_tensor(
                            nr[0:1, 36 + part:37 + part], nr[0:1, 36 + part:37 + part], nr[0:1, 34 + part:35 + part], ALU.max), deps=[er])
                    nevs.append((em, er))
        bw[3] = nevs[-1][1]
        e_n1 = p.op("scalar", lambda s: s.activation(nr[0:1, 36:37], nr[0:1, 36:37], AF.Sqrt), deps=[nevs[-1][1]])
        e_n2 = p.op("scalar", lambda s: s.activation(nr[0:1, 37:38], nr[0:1, 37:38], AF.Sqrt), deps=[e_n1])
        e_n3 = p.op("vector", lambda v: v.tensor_tensor(nr[0:1, 38:39], nr[0:1, 36:37], nr[0:1, 37:38], ALU.mult), deps=[e_n2])
        e_n4 = p.op("vector", lambda v: v.tensor_scalar(nr[0:1, 38:39], nr[0:1, 38:39], -1.02, None, ALU.mult), deps=[e_n3])
        nc_bf = A3[0:1, 2051:2052]
        e_n5 = p.op("vector", lambda v: v.tensor_copy(nc_bf, nr[0:1, 38:39]), deps=[e_n4])
        e_n6 = p.op("tensor", lambda tt: tt.transpose(
            cx.pbt[1][:, 0:1], nc_bf.broadcast_to((1, 128)), cx.eye_bf[0:1, 0:1]),
            deps=[e_n5, cx.ev_eye, bw.get(("t", 1))])
        e_negc = p.op("vector", lambda v: v.tensor_copy(negc, cx.pbt[1][:, 0:1]), deps=[e_n6])
        bw[("t", 1)] = e_negc

        gl_evs = []
        for t in range(NT):
            bank = cx.pb[3]
            gms = []
            for gi, (kb2, col) in enumerate(((0, 0), (0, 1), (64, 0), (64, 1))):
                war = [bw.get(3)] if gi == 0 else []
                gm = p.op("tensor", lambda tt, t=t, gi=gi, kb2=kb2, col=col, bank=bank, hp=hp: tt.matmul(
                    bank[:, 400 + gi:401 + gi], qf[kb2:kb2 + 64, t * 128:(t + 1) * 128],
                    glw_v[kb2:kb2 + 64, hp, col:col + 1], start=True, stop=True),
                    deps=[qk_done[-1][0], e_glw8] + war + gms[-1:])
                gms.append(gm)
            e_sg = p.op("scalar", lambda s, t=t, bank=bank: s.activation(glbuf[:, t * 4:t * 4 + 4], bank[:, 400:404], AF.Sigmoid), deps=gms)
            bw[3] = e_sg
            gl_evs.append(e_sg)

        exp_by_buf = {}
        pv_last_for_buf = {}
        po_banks = (cx.pb[4], cx.pb[5])
        for head in range(2):
            kbase = 64 * head
            hgl = 2 * hp + head
            po = po_banks[head]
            vtile = vt if head == 0 else vt2
            for qg in range(4):
                kts = [k for k in range(max(0, 4 * qg - 2), 4 * qg + 4)]
                pv_evs = []
                for ki, kt in enumerate(kts):
                    bi = 2 * head + (ki % 2)
                    lb = cx.pb[bi][:, 0:512]
                    pbi = 2 * head + (ki % 2)
                    pbuf = pP[pbi]
                    war_l = [exp_by_buf.get(bi), bw.get(bi)]
                    m1 = p.op("tensor", lambda tt, kt=kt, qg=qg, lb=lb, kbase=kbase: tt.matmul(
                        lb, kf[kbase:kbase + 64, kt * 128:(kt + 1) * 128],
                        qf[kbase:kbase + 64, qg * 512:(qg + 1) * 512], start=True, stop=False),
                        deps=[qk_done[-1][0], qk_done[-1][1]] + war_l)
                    soff = qg * 512 - kt * 128 + 384
                    m2 = p.op("tensor", lambda tt, lb=lb, soff=soff: tt.matmul(
                        lb, cx.J_bf[:, :], stripw[:, soff:soff + 512], start=False, stop=True),
                        deps=[ev_stw, cx.ev_J, m1])
                    war_p = [pv_last_for_buf.get(pbi)]
                    e_exp = p.op("scalar", lambda s, lb=lb, pbuf=pbuf: s.activation(
                        pbuf, lb, AF.Exp, bias=negc[:, 0:1]), deps=[m2, e_negc] + war_p)
                    exp_by_buf[bi] = e_exp
                    for sub in range(4):
                        st_ = (ki == 0 and sub == 0)
                        sp = (ki == len(kts) - 1)
                        dep_po = [bw.get(4 + head)] if st_ else []
                        mm = p.op("tensor", lambda tt, pbuf=pbuf, sub=sub, kt=kt, vtile=vtile, po=po, st_=st_, sp=sp: tt.matmul(
                            po[:, sub * 68:sub * 68 + 65], pbuf[:, sub * 128:(sub + 1) * 128],
                            vtile[:, kt, 0:65], start=st_, stop=sp),
                            deps=[e_exp, v_done[kt][0], v_done[kt][1]] + dep_po)
                        pv_evs.append(mm)
                    pv_last_for_buf[pbi] = pv_evs[-1]
                e_prev = None
                for sub in range(4):
                    tq_idx = qg * 4 + sub
                    dn = nr[:, 44 + head:45 + head]
                    e_g = p.op("vector", lambda v, po=po, sub=sub, dn=dn: v.tensor_scalar(
                        dn, po[:, sub * 68 + 64:sub * 68 + 65], 1e-38, None, ALU.max),
                        deps=pv_evs[-4:] + ([e_prev] if e_prev else []))
                    e_rc = p.op("vector", lambda v, dn=dn: v.reciprocal(dn, dn), deps=[e_g])
                    e_o = p.op("vector", lambda v, po=po, sub=sub, tq_idx=tq_idx, hgl=hgl, dn=dn: v.tensor_scalar(
                        o_tm[:, tq_idx, hgl * 64:hgl * 64 + 64],
                        po[:, sub * 68:sub * 68 + 64], dn[:, 0:1], None, ALU.mult),
                        deps=[e_rc] + rem_evs[-1:])
                    rms = []
                    for c2 in range(4):
                        war_r = [bw.get(3)] if c2 == 0 else []
                        rm = p.op("tensor", lambda tt, c2=c2, tq_idx=tq_idx, head=head: tt.matmul(
                            cx.pb[3][:, 0:64], rem128[:, c2, tq_idx * 128:(tq_idx + 1) * 128],
                            rw_sl[:, c2, head * 64:head * 64 + 64], start=(c2 == 0), stop=(c2 == 3)),
                            deps=[e_rw] + rem_evs[-1:] + war_r + rms[-1:])
                        rms.append(rm)
                    gli = tq_idx * 4 + 2 * head
                    e_m1 = p.op("vector", lambda v, tq_idx=tq_idx, hgl=hgl, gli=gli: v.tensor_scalar(
                        o_tm[:, tq_idx, hgl * 64:hgl * 64 + 64],
                        o_tm[:, tq_idx, hgl * 64:hgl * 64 + 64],
                        glbuf[:, gli:gli + 1], None, ALU.mult),
                        deps=[e_o, gl_evs[tq_idx]])
                    e_m2 = p.op("vector", lambda v, gli=gli: v.tensor_scalar(
                        cx.sA[:, 64:128], cx.pb[3][:, 0:64], glbuf[:, gli + 1:gli + 2], None, ALU.mult),
                        deps=[rms[-1], gl_evs[tq_idx], e_m1])
                    e_m3 = p.op("vector", lambda v, tq_idx=tq_idx, hgl=hgl: v.tensor_tensor(
                        o_tm[:, tq_idx, hgl * 64:hgl * 64 + 64],
                        o_tm[:, tq_idx, hgl * 64:hgl * 64 + 64],
                        cx.sA[:, 64:128], ALU.add), deps=[e_m2])
                    bw[3] = e_m3
                    e_prev = e_m3
                bw[4 + head] = e_prev
        for bi in range(4):
            if exp_by_buf.get(bi) is not None:
                bw[bi] = exp_by_buf[bi]
        prev_hp = [bw[4], bw[5], bw[3]] + [exp_by_buf.get(b) for b in range(4)]
    o_done = [e for e in prev_hp if e is not None]

    wo_sb = A3[:, 4096:12288].rearrange("p (c w) -> p c w", c=NCH)
    wo_evs = [p.dma(cx.ch_wg, lambda e, c=c: e.dma_start(
        out=wo_sb[:, c, :], in_=d["l1_wout"][c * 128:(c + 1) * 128, :]), deps=o_done) for c in range(NCH)]
    stg = [A3[:, 2304 + i * 128:2304 + (i + 1) * 128] for i in range(2)]
    for qt in range(NT):
        b_out = (cx.pb[4], cx.pb[5])
        mm_last = [None, None]
        for c in range(NCH):
            bank = cx.pbt[c % 2]
            e_t = p.op("tensor", lambda tt, qt=qt, c=c, bank=bank: tt.transpose(
                bank[:, 0:128], o_tm[:, qt, c * 128:(c + 1) * 128], cx.eye_bf[:, :]),
                deps=o_done + [bw.get(("t", c % 2)), cx.ev_eye])
            e_s = p.op("vector", lambda v, c=c, bank=bank: v.tensor_copy(stg[c % 2][:, 0:128], bank[:, 0:128]),
                       deps=[e_t] + ([mm_last[1]] if c >= 2 else [bw.get(("stg", c % 2))]))
            bw[("t", c % 2)] = e_s
            for half in range(2):
                st_ = (c == 0)
                war_o = [bw.get(4 + half)] if c == 0 else []
                mm = p.op("tensor", lambda tt, c=c, half=half, qt=qt, st_=st_: tt.matmul(
                    b_out[half][:, 0:512], stg[c % 2][:, 0:128], wo_sb[:, c, half * 512:(half + 1) * 512],
                    start=st_, stop=(c == NCH - 1)), deps=[e_s, wo_evs[c]] + war_o)
                mm_last[half] = mm
            bw[("stg", c % 2)] = mm_last[1]
        for half in range(2):
            e_x = p.op("vector", lambda v, qt=qt, half=half, b_out=b_out: v.tensor_tensor(
                cx.x_sb[:, qt, half * 512:(half + 1) * 512], cx.x_sb[:, qt, half * 512:(half + 1) * 512],
                b_out[half][:, 0:512], ALU.add), deps=[mm_last[half]])
            bw[4 + half] = e_x
            x_upd.append(e_x)

    if "nol1ffn" in STAGES:
        return x_upd

    def bcast_scale(src_col, dst_s, dst_r, deps):
        e_b = p.op("vector", lambda v: v.tensor_copy(A3[0:1, 2052:2053], src_col), deps=deps)
        bank = cx.pb[3]
        e_m = p.op("tensor", lambda tt: tt.matmul(
            bank[:, 448:449], ones_r[0:1, 0:128],
            A3[0:1, 2052:2053], start=True, stop=True), deps=[e_b, e_or, bw.get(3)])
        e_c = p.op("vector", lambda v: v.tensor_copy(nr[:, 58:59], bank[:, 448:449]), deps=[e_m])
        bw[3] = e_c
        e_s = p.op("vector", lambda v: v.tensor_scalar(dst_s, nr[:, 58:59], 1.0 / 32767.0, 1e-12, ALU.mult, ALU.max), deps=[e_c])
        e_r = p.op("vector", lambda v: v.reciprocal(dst_r, dst_s), deps=[e_s])
        return e_r
    ones_r = A2[:, 12800:12928]
    e_or = p.op("vector", lambda v: v.tensor_scalar(ones_r[0:1, 0:128], cx.ones_f[0:1, 0:1].broadcast_to((1, 128)), 1.0, None, ALU.mult), deps=[cx.ev_ones] + x_upd[-2:])
    e_sxn = bcast_scale(nr[0:1, 50:51], nr[:, 48:49], nr[:, 49:50], [ev_co1] + x_upd[-2:])

    scrf = A2[:, 0:4096].bitcast(F32)
    xq_evs = []
    for c in range(NCH):
        dep0 = [e_sxn] + last_xnf_reader + xq_evs[-1:]
        q1 = p.op("vector", lambda v, c=c: v.tensor_scalar(scrf[:, 0:2048], cx.xnf[:, c, :], nr[:, 49:50], cx.ceff[:, 0:1], ALU.mult, ALU.add), deps=dep0 + [cx.ev_ceff])
        q2 = p.op("vector", lambda v: v.tensor_scalar(scrf[:, 0:2048], scrf[:, 0:2048], cx.ceff[:, 0:1], None, ALU.subtract), deps=[q1])
        q3 = p.op("vector", lambda v: v.tensor_scalar(scrf[:, 0:2048], scrf[:, 0:2048], 32767.0, -32768.0, ALU.min, ALU.max), deps=[q2])
        q4 = p.op("vector", lambda v, c=c: v.tensor_scalar(cx.xnf[:, c, :], scrf[:, 0:2048], nr[:, 48:49], None, ALU.mult), deps=[q3])
        xq_evs.append(q4)

    def lpbq(buf_f32, nb, deps):
        x3 = buf_f32.rearrange("p (a b) -> p a b", b=32)
        bm = cx.fp4_bm[:, 0:nb]
        rinv = cx.fp4_rinv[:, 0:nb]
        e1 = p.op("vector", lambda v: v.tensor_reduce(bm, x3, axis=AX.X, op=ALU.max, apply_absolute_value=True), deps=deps)
        e2 = p.op("vector", lambda v: v.tensor_scalar(bm, bm, 1.0 / 7.0, 1e-12, ALU.mult, ALU.max), deps=[e1])
        e3 = p.op("vector", lambda v: v.reciprocal(rinv, bm), deps=[e2])
        rb = rinv.unsqueeze(2).broadcast_to((PT, nb, 32))
        e4 = p.op("vector", lambda v: v.tensor_tensor(x3, x3, rb, ALU.mult), deps=[e3])
        e5 = p.op("vector", lambda v: v.tensor_scalar(buf_f32, buf_f32, cx.ceff[:, 0:1], None, ALU.add), deps=[e4, cx.ev_ceff])
        e6 = p.op("vector", lambda v: v.tensor_scalar(buf_f32, buf_f32, cx.ceff[:, 0:1], None, ALU.subtract), deps=[e5])
        e7 = p.op("vector", lambda v: v.tensor_scalar(buf_f32, buf_f32, 7.0, -8.0, ALU.min, ALU.max), deps=[e6])
        bb = bm.unsqueeze(2).broadcast_to((PT, nb, 32))
        e8 = p.op("vector", lambda v: v.tensor_tensor(x3, x3, bb, ALU.mult), deps=[e7])
        return e8
    wstage = [cx.sB[:, 0:1024], cx.sC[:, 0:1024]]
    wT = [A2[:, 4096 + i * 1024:4096 + (i + 1) * 1024].rearrange("p (c w) -> p c w", c=8) for i in range(2)]
    gst = A2[:, 6144:6656]
    ust = A2[:, 6656:7168]
    gu_mx_evs = []
    e_mxg = e_mxu = None
    prev_round = {}
    for hc in range(24):
        for wi, wn in enumerate(("l1_wg", "l1_wu")):
            ev_w = p.dma(cx.ch_w, lambda e, hc=hc, wn=wn, wi=wi: e.dma_start(
                out=wstage[wi][:, 0:1024], in_=d[wn][hc * 128:(hc + 1) * 128, :]),
                deps=[prev_round.get(("q", wi))] + ([e_sxn] if hc == 0 else []))
            e_dq = lpbq(wstage[wi], 32, [ev_w] + ([stat_evs[-1]] if hc == 0 else []))
            wqbf = A2[:, 7168 + wi * 1024:7168 + (wi + 1) * 1024]
            e_bf = p.op("vector", lambda v, wi=wi, wqbf=wqbf: v.tensor_copy(wqbf, wstage[wi][:, 0:1024]),
                        deps=[e_dq, prev_round.get(("tT", wi))])
            tevs = []
            for ic in range(8):
                bank = cx.pbt[wi]
                e_t = p.op("tensor", lambda tt, wi=wi, ic=ic, bank=bank, wqbf=wqbf: tt.transpose(
                    bank[:, 0:128], wqbf[:, ic * 128:(ic + 1) * 128], cx.eye_bf[:, :]),
                    deps=[e_bf, cx.ev_eye, bw.get(("t", wi))])
                e_c = p.op("vector", lambda v, wi=wi, ic=ic, bank=bank: v.tensor_copy(wT[wi][:, ic, :], bank[:, 0:128]),
                           deps=[e_t, prev_round.get(("mm", wi))])
                bw[("t", wi)] = e_c
                tevs.append(e_c)
            prev_round[("tT", wi)] = tevs[-1]
            prev_round[("q", wi)] = tevs[-1]
            for tb in range(4):
                bank = cx.pb[wi]
                mm = None
                for ic in range(8):
                    war = [bw.get(wi)] if ic == 0 else []
                    mm = p.op("tensor", lambda tt, wi=wi, ic=ic, tb=tb, bank=bank: tt.matmul(
                        bank[:, 0:512], wT[wi][:, ic, :], cx.xnf[:, ic, tb * 512:(tb + 1) * 512],
                        start=(ic == 0), stop=(ic == 7)), deps=[tevs[ic], xq_evs[ic]] + war)
                prev_round[("mm", wi)] = mm
                stb = gst if wi == 0 else ust
                e_mx_prev = e_mxg if wi == 0 else e_mxu
                e_m = p.op("vector", lambda v, wi=wi, bank=bank: v.tensor_reduce(
                    nr[:, 60 + wi:61 + wi], bank[:, 0:512], axis=AX.X, op=ALU.max, apply_absolute_value=True),
                    deps=[mm] + ([e_mx_prev] if e_mx_prev else []))
                if hc == 0 and tb == 0:
                    e_m2 = p.op("vector", lambda v, wi=wi: v.tensor_scalar(
                        nr[:, 62 + wi:63 + wi], nr[:, 60 + wi:61 + wi], 1.0, None, ALU.mult), deps=[e_m])
                else:
                    e_m2 = p.op("vector", lambda v, wi=wi: v.tensor_tensor(
                        nr[:, 62 + wi:63 + wi], nr[:, 62 + wi:63 + wi], nr[:, 60 + wi:61 + wi], ALU.max), deps=[e_m])
                if wi == 0:
                    e_mxg = e_m2
                else:
                    e_mxu = e_m2
                e_st = p.op("vector", lambda v, wi=wi, bank=bank, stb=stb: v.tensor_copy(stb[:, 0:512], bank[:, 0:512]),
                            deps=[e_m, prev_round.get(("st", wi))])
                bw[wi] = e_st
                scr = cx.scr_g if wi == 0 else cx.scr_u
                e_sd = p.dma(cx.ch_s, lambda e, hc=hc, tb=tb, scr=scr, stb=stb: e.dma_start(
                    out=scr[:, hc * 2048 + tb * 512:hc * 2048 + (tb + 1) * 512], in_=stb[:, 0:512]), deps=[e_st])
                prev_round[("st", wi)] = e_sd
    e_p1 = p.op("vector", lambda v: v.tensor_copy(A3[:, 2053:2054], nr[:, 62:63]), deps=[e_mxg])
    e_p2 = p.op("vector", lambda v: v.tensor_copy(A3[:, 2054:2055], nr[:, 63:64]), deps=[e_mxu])
    e_t1 = p.op("tensor", lambda tt: tt.transpose(
        cx.pbt[1][:, 0:128], A3[:, 2053:2054].broadcast_to((PT, 128)), cx.eye_bf[:, :]),
        deps=[e_p1, cx.ev_eye, bw.get(("t", 1))])
    e_r1 = p.op("vector", lambda v: v.tensor_reduce(nr[0:1, 56:57], cx.pbt[1][0:1, 0:128], axis=AX.X, op=ALU.max), deps=[e_t1])
    e_t2 = p.op("tensor", lambda tt: tt.transpose(
        cx.pbt[1][:, 0:128], A3[:, 2054:2055].broadcast_to((PT, 128)), cx.eye_bf[:, :]),
        deps=[e_p2, cx.ev_eye, e_r1])
    e_r2 = p.op("vector", lambda v: v.tensor_reduce(nr[0:1, 57:58], cx.pbt[1][0:1, 0:128], axis=AX.X, op=ALU.max), deps=[e_t2])
    bw[("t", 1)] = e_r2
    ev_ci2 = p.dma(cx.ch_m, lambda e: e.dma_start(out=d["cc_i2"], in_=nr[0:1, 56:58]), deps=[e_r1, e_r2])
    ev_cc2 = p.op("gpsimd", lambda g: g.collective_compute(
        "AllReduce", ALU.max, replica_groups=[list(range(NCORES))],
        ins=[cx.cc_i[1].ap().opt()], outs=[cx.cc_o[1].ap().opt()]), deps=[ev_ci2])
    ev_co2 = p.dma(cx.ch_m, lambda e: e.dma_start(out=nr[0:1, 56:58], in_=d["cc_o2"]), deps=[ev_cc2])
    e_sg = bcast_scale(nr[0:1, 56:57], nr[:, 50:51], nr[:, 51:52], [ev_co2])
    e_su = bcast_scale(nr[0:1, 57:58], nr[:, 52:53], nr[:, 53:54], [e_sg])

    A3f = A3[:, 4096:12288].bitcast(F32)
    g2b = A2[:, 7168:9216]
    e_mx2 = None
    prev_ld = None
    for hc in range(24):
        ev_lg = p.dma(cx.ch_wg, lambda e, hc=hc: e.dma_start(
            out=A2[:, 0:2048], in_=cx.scr_g[:, hc * 2048:(hc + 1) * 2048]),
            deps=[prev_round[("st", 0)], e_sg, prev_ld])
        f1 = p.op("vector", lambda v: v.tensor_scalar(A3f[:, 0:2048], A2[:, 0:2048], nr[:, 51:52], cx.ceff[:, 0:1], ALU.mult, ALU.add), deps=[ev_lg, e_sg, cx.ev_ceff] + ([e_mx2] if e_mx2 else []))
        f2 = p.op("vector", lambda v: v.tensor_scalar(A3f[:, 0:2048], A3f[:, 0:2048], cx.ceff[:, 0:1], None, ALU.subtract), deps=[f1])
        f3 = p.op("vector", lambda v: v.tensor_scalar(A3f[:, 0:2048], A3f[:, 0:2048], 32767.0, -32768.0, ALU.min, ALU.max), deps=[f2])
        f4 = p.op("vector", lambda v: v.tensor_scalar(A3f[:, 0:2048], A3f[:, 0:2048], nr[:, 50:51], None, ALU.mult), deps=[f3])
        prev_ld = f4
        s1 = p.op("scalar", lambda s: s.activation(A3f[:, 2048:4096], A3f[:, 0:2048], AF.Sigmoid), deps=[f4])
        s2 = p.op("vector", lambda v: v.tensor_scalar(A3f[:, 2048:4096], A3f[:, 2048:4096], 65536.0, cx.ceff[:, 0:1], ALU.mult, ALU.add), deps=[s1])
        s3 = p.op("vector", lambda v: v.tensor_scalar(A3f[:, 2048:4096], A3f[:, 2048:4096], cx.ceff[:, 0:1], None, ALU.subtract), deps=[s2])
        s4 = p.op("vector", lambda v: v.tensor_scalar(A3f[:, 2048:4096], A3f[:, 2048:4096], 65535.0, 0.0, ALU.min, ALU.max), deps=[s3])
        s5 = p.op("vector", lambda v: v.tensor_scalar(A3f[:, 2048:4096], A3f[:, 2048:4096], 1.0 / 65536.0, None, ALU.mult), deps=[s4])
        if "l1p2" in DBG and hc == 0:
            dp2 = cx.nc.dram_tensor("y_ph2", [PT, 4096], F32, kind="ExternalOutput").ap()
            ev_d = p.dma(cx.ch_m, lambda e: e.dma_start(out=dp2[:, 0:2048], in_=A3f[:, 0:2048]), deps=[f4, s5])
            ev_d2 = p.dma(cx.ch_m, lambda e: e.dma_start(out=dp2[:, 2048:4096], in_=A3f[:, 2048:4096]), deps=[f4, s5, ev_d])
            cx.dbg_evs.append(ev_d2)
            g1 = p.op("vector", lambda v: v.tensor_tensor(A3f[:, 0:2048], A3f[:, 0:2048], A3f[:, 2048:4096], ALU.mult), deps=[s5, ev_d2])
        else:
            g1 = p.op("vector", lambda v: v.tensor_tensor(A3f[:, 0:2048], A3f[:, 0:2048], A3f[:, 2048:4096], ALU.mult), deps=[s5])
        e_m = p.op("vector", lambda v: v.tensor_reduce(nr[:, 60:61], A3f[:, 0:2048], axis=AX.X, op=ALU.max, apply_absolute_value=True), deps=[g1])
        if hc == 0:
            e_mx2 = p.op("vector", lambda v: v.tensor_scalar(nr[:, 62:63], nr[:, 60:61], 1.0, None, ALU.mult), deps=[e_m])
        else:
            e_mx2 = p.op("vector", lambda v: v.tensor_tensor(nr[:, 62:63], nr[:, 62:63], nr[:, 60:61], ALU.max), deps=[e_m])
        e_cb = p.op("vector", lambda v: v.tensor_copy(g2b[:, 0:2048], A3f[:, 0:2048]), deps=[g1, prev_round.get("g2d")])
        e_sd = p.dma(cx.ch_s, lambda e, hc=hc: e.dma_start(
            out=cx.scr_o[:, hc * 2048:(hc + 1) * 2048], in_=g2b[:, 0:2048]), deps=[e_cb, prev_round[("st", 1)] if hc == 0 else None])
        prev_round["g2d"] = e_sd
    e_p3 = p.op("vector", lambda v: v.tensor_copy(A3[:, 2053:2054], nr[:, 62:63]), deps=[e_mx2])
    e_t3 = p.op("tensor", lambda tt: tt.transpose(
        cx.pbt[1][:, 0:128], A3[:, 2053:2054].broadcast_to((PT, 128)), cx.eye_bf[:, :]),
        deps=[e_p3, cx.ev_eye, bw.get(("t", 1))])
    e_r3 = p.op("vector", lambda v: v.tensor_reduce(nr[0:1, 56:57], cx.pbt[1][0:1, 0:128], axis=AX.X, op=ALU.max), deps=[e_t3])
    bw[("t", 1)] = e_r3
    ev_ci3 = p.dma(cx.ch_m, lambda e: e.dma_start(out=d["cc_i3"], in_=nr[0:1, 56:58]), deps=[e_r3])
    ev_cc3 = p.op("gpsimd", lambda g: g.collective_compute(
        "AllReduce", ALU.max, replica_groups=[list(range(NCORES))],
        ins=[cx.cc_i[2].ap().opt()], outs=[cx.cc_o[2].ap().opt()]), deps=[ev_ci3])
    ev_co3 = p.dma(cx.ch_m, lambda e: e.dma_start(out=nr[0:1, 56:58], in_=d["cc_o3"]), deps=[ev_cc3])
    e_sg2 = bcast_scale(nr[0:1, 56:57], nr[:, 54:55], nr[:, 55:56], [ev_co3])

    if "l1p3" in DBG:
        dsm2 = cx.nc.dram_tensor("y_small2", [PT, 64], F32, kind="ExternalOutput").ap()
        ev_ds2 = p.dma(cx.ch_m, lambda e: e.dma_start(out=dsm2, in_=nr[:, 0:64]), deps=[e_sg2])
        cx.dbg_evs.append(ev_ds2)
        e_sg2 = ev_ds2
    e_mx3 = None
    prev_ld2 = None
    for hc in range(24):
        ev_l2 = p.dma(cx.ch_wg, lambda e, hc=hc: e.dma_start(
            out=A2[:, 0:2048], in_=cx.scr_o[:, hc * 2048:(hc + 1) * 2048]),
            deps=[prev_round["g2d"], e_sg2, prev_ld2])
        ev_lu = p.dma(cx.ch_wg, lambda e, hc=hc: e.dma_start(
            out=A2[:, 2048:4096], in_=cx.scr_u[:, hc * 2048:(hc + 1) * 2048]),
            deps=[prev_round[("st", 1)], e_su, prev_ld2])
        h1 = p.op("vector", lambda v: v.tensor_scalar(A3f[:, 0:2048], A2[:, 0:2048], nr[:, 55:56], cx.ceff[:, 0:1], ALU.mult, ALU.add), deps=[ev_l2, e_sg2] + ([e_mx3] if e_mx3 else []))
        h2 = p.op("vector", lambda v: v.tensor_scalar(A3f[:, 0:2048], A3f[:, 0:2048], cx.ceff[:, 0:1], None, ALU.subtract), deps=[h1])
        h3 = p.op("vector", lambda v: v.tensor_scalar(A3f[:, 0:2048], A3f[:, 0:2048], 32767.0, -32768.0, ALU.min, ALU.max), deps=[h2])
        h4 = p.op("vector", lambda v: v.tensor_scalar(A3f[:, 0:2048], A3f[:, 0:2048], nr[:, 54:55], None, ALU.mult), deps=[h3])
        u1 = p.op("vector", lambda v: v.tensor_scalar(A3f[:, 2048:4096], A2[:, 2048:4096], nr[:, 53:54], cx.ceff[:, 0:1], ALU.mult, ALU.add), deps=[ev_lu, e_su, h4])
        u2 = p.op("vector", lambda v: v.tensor_scalar(A3f[:, 2048:4096], A3f[:, 2048:4096], cx.ceff[:, 0:1], None, ALU.subtract), deps=[u1])
        u3 = p.op("vector", lambda v: v.tensor_scalar(A3f[:, 2048:4096], A3f[:, 2048:4096], 32767.0, -32768.0, ALU.min, ALU.max), deps=[u2])
        u4 = p.op("vector", lambda v: v.tensor_scalar(A3f[:, 2048:4096], A3f[:, 2048:4096], nr[:, 52:53], None, ALU.mult), deps=[u3])
        o1 = p.op("vector", lambda v: v.tensor_tensor(A3f[:, 0:2048], A3f[:, 0:2048], A3f[:, 2048:4096], ALU.mult), deps=[u4])
        e_m = p.op("vector", lambda v: v.tensor_reduce(nr[:, 60:61], A3f[:, 0:2048], axis=AX.X, op=ALU.max, apply_absolute_value=True), deps=[o1])
        if hc == 0:
            e_mx3 = p.op("vector", lambda v: v.tensor_scalar(nr[:, 62:63], nr[:, 60:61], 1.0, None, ALU.mult), deps=[e_m])
        else:
            e_mx3 = p.op("vector", lambda v: v.tensor_tensor(nr[:, 62:63], nr[:, 62:63], nr[:, 60:61], ALU.max), deps=[e_m])
        prev_ld2 = e_mx3
        e_cb = p.op("vector", lambda v: v.tensor_copy(g2b[:, 0:2048], A3f[:, 0:2048]), deps=[o1, prev_round.get("od")])
        e_sd = p.dma(cx.ch_s, lambda e, hc=hc: e.dma_start(
            out=cx.scr_g[:, hc * 2048:(hc + 1) * 2048], in_=g2b[:, 0:2048]), deps=[e_cb])
        prev_round["od"] = e_sd
    e_p4 = p.op("vector", lambda v: v.tensor_copy(A3[:, 2053:2054], nr[:, 62:63]), deps=[e_mx3])
    e_t4 = p.op("tensor", lambda tt: tt.transpose(
        cx.pbt[1][:, 0:128], A3[:, 2053:2054].broadcast_to((PT, 128)), cx.eye_bf[:, :]),
        deps=[e_p4, cx.ev_eye, bw.get(("t", 1))])
    e_r4 = p.op("vector", lambda v: v.tensor_reduce(nr[0:1, 56:57], cx.pbt[1][0:1, 0:128], axis=AX.X, op=ALU.max), deps=[e_t4])
    bw[("t", 1)] = e_r4
    ev_ci4 = p.dma(cx.ch_m, lambda e: e.dma_start(out=d["cc_i4"], in_=nr[0:1, 56:58]), deps=[e_r4])
    ev_cc4 = p.op("gpsimd", lambda g: g.collective_compute(
        "AllReduce", ALU.max, replica_groups=[list(range(NCORES))],
        ins=[cx.cc_i[3].ap().opt()], outs=[cx.cc_o[3].ap().opt()]), deps=[ev_ci4])
    ev_co4 = p.dma(cx.ch_m, lambda e: e.dma_start(out=nr[0:1, 56:58], in_=d["cc_o4"]), deps=[ev_cc4])
    e_so = bcast_scale(nr[0:1, 56:57], nr[:, 52:53], nr[:, 53:54], [ev_co4])

    if "l1f" in DBG:
        dsm = cx.nc.dram_tensor("y_small", [PT, 64], F32, kind="ExternalOutput").ap()
        cx.dbg_evs.append(p.dma(cx.ch_m, lambda e: e.dma_start(out=dsm, in_=nr[:, 0:64]), deps=[e_so, e_sg2]))
        dstash = cx.nc.dram_tensor("y_gstash", [PT, 2048], BF16, kind="ExternalOutput").ap()
        cx.dbg_evs.append(p.dma(cx.ch_m, lambda e: e.dma_start(out=dstash, in_=cx.scr_u[:, 0:2048]), deps=[prev_round[("st", 1)], e_sg]))
    xnf_flat = cx.xnf[:, :, :].rearrange("p c t -> p (c t)")

    def wdT_ap(hc):
        if hc < 16:
            return A1[:, hc * 1024:(hc + 1) * 1024]
        return xnf_flat[:, (hc - 16) * 1024:(hc - 15) * 1024]
    wd_evs = {}
    prev_oc = []
    for oc in range(8):
        ev_w = p.dma(cx.ch_w, lambda e, oc=oc: e.dma_start(
            out=cx.sB[:, 0:1472] if False else cx.sB[:, 0:1024], in_=d["l1_wd"][oc * 128:(oc + 1) * 128, 0:1024]),
            deps=[prev_round[("q", 0)], prev_round[("q", 1)], prev_round[("mm", 0)], prev_round[("mm", 1)]] + prev_oc + x_upd[-2:])
        ev_w2 = p.dma(cx.ch_w, lambda e, oc=oc: e.dma_start(
            out=cx.sC[:, 0:1024], in_=d["l1_wd"][oc * 128:(oc + 1) * 128, 1024:2048]), deps=[ev_w])
        ev_w3 = p.dma(cx.ch_w, lambda e, oc=oc: e.dma_start(
            out=cx.sA[:, 0:1024], in_=d["l1_wd"][oc * 128:(oc + 1) * 128, 2048:3072]), deps=[ev_w2])
        prev_oc = []
        wdbf = A2[:, 4096:5120]
        for wi, buf in enumerate((cx.sB[:, 0:1024], cx.sC[:, 0:1024], cx.sA[:, 0:1024])):
            e_dq = lpbq(buf, 32, [ev_w, ev_w2, ev_w3] + prev_oc)
            e_bf = p.op("vector", lambda v, buf=buf: v.tensor_copy(wdbf, buf[:, 0:1024]),
                        deps=[e_dq, wd_evs.get("tlast"), prev_round[("mm", 0)], prev_round[("mm", 1)]])
            for ic in range(8):
                hc = wi * 8 + ic
                bank = cx.pbt[0]
                e_t = p.op("tensor", lambda tt, ic=ic, bank=bank: tt.transpose(
                    bank[:, 0:128], wdbf[:, ic * 128:(ic + 1) * 128], cx.eye_bf[:, :]),
                    deps=[e_bf, cx.ev_eye, bw.get(("t", 0))])
                e_c = p.op("vector", lambda v, hc=hc, oc=oc, bank=bank: v.tensor_copy(
                    wdT_ap(hc)[:, oc * 128:(oc + 1) * 128], bank[:, 0:128]), deps=[e_t])
                bw[("t", 0)] = e_c
                wd_evs[(hc, oc)] = e_c
                wd_evs["tlast"] = e_t
            prev_oc.append(e_c)

    ob = A2[:, 8192:11264]
    obq = A2[:, 11264:14336]
    scrf2 = A2[:, 0:6144].bitcast(F32)
    x2_upd = []
    prev_t = []
    for t in range(NT):
        ev_lo = p.dma(cx.ch_wg, lambda e, t=t: e.dma_start(
            out=ob[:, 0:3072], in_=bass.AP(cx.scr_g.tensor, t * 128, [[49152, 128], [2048, 24], [1, 128]])),
            deps=[prev_round["od"], e_so, wd_evs[(23, 7)]] + prev_t)
        d1 = p.op("vector", lambda v: v.tensor_scalar(scrf2[:, 0:1536], ob[:, 0:1536], nr[:, 53:54], cx.ceff[:, 0:1], ALU.mult, ALU.add), deps=[ev_lo, e_so] + prev_t)
        d1b = p.op("vector", lambda v: v.tensor_scalar(scrf2[:, 1536:3072], ob[:, 1536:3072], nr[:, 53:54], cx.ceff[:, 0:1], ALU.mult, ALU.add), deps=[d1])
        d2 = p.op("vector", lambda v: v.tensor_scalar(scrf2[:, 0:3072], scrf2[:, 0:3072], cx.ceff[:, 0:1], None, ALU.subtract), deps=[d1b])
        d3 = p.op("vector", lambda v: v.tensor_scalar(scrf2[:, 0:3072], scrf2[:, 0:3072], 32767.0, -32768.0, ALU.min, ALU.max), deps=[d2])
        d4 = p.op("vector", lambda v: v.tensor_scalar(obq[:, 0:3072], scrf2[:, 0:3072], nr[:, 52:53], None, ALU.mult), deps=[d3])
        obq3 = obq[:, 0:3072].rearrange("p (c w) -> p c w", c=24)
        mm_l = [None, None]
        for hc in range(24):
            for half in range(2):
                bank = cx.pb[4 + half]
                war = [bw.get(4 + half)] if hc == 0 else []
                mm = p.op("tensor", lambda tt, hc=hc, half=half, bank=bank, obq3=obq3: tt.matmul(
                    bank[:, 0:512], obq3[:, hc, :], wdT_ap(hc)[:, half * 512:(half + 1) * 512],
                    start=(hc == 0), stop=(hc == 23)),
                    deps=[d4, wd_evs[(hc, 7)]] + war + ([mm_l[half]] if mm_l[half] else []))
                mm_l[half] = mm
        prev_t = []
        for half in range(2):
            e_x2 = p.op("vector", lambda v, t=t, half=half: v.tensor_tensor(
                cx.x_sb[:, t, half * 512:(half + 1) * 512], cx.x_sb[:, t, half * 512:(half + 1) * 512],
                cx.pb[4 + half][:, 0:512], ALU.add), deps=[mm_l[half], x_upd[2 * t + half]])
            bw[4 + half] = e_x2
            x2_upd.append(e_x2)
            prev_t.append(e_x2)
    return x2_upd


def emit_layer2(cx, stat_evs, xnf_evs):
    p, d = cx.p, cx.dram
    A1, A2, A3 = cx.AR1, cx.AR2, cx.AR3
    nr = cx.small
    bw = cx.bw
    NC2 = 16

    xn2f = A1[:, 0:16384].rearrange("p (c t) -> p c t", c=NCH)
    ln2_evs, xn2_evs = emit_ln(cx, list(stat_evs) + list(xnf_evs), double=True, dst=xn2f, st_off=2 * NT)
    ev_lnw = p.dma(cx.ch_m, lambda e: e.dma_start(out=cx.sC[:, 32:40], in_=d["l2_lnw"]), deps=list(stat_evs))
    ev_lnb = p.dma(cx.ch_m, lambda e: e.dma_start(out=cx.sC[:, 40:48], in_=d["l2_lnb"]), deps=[ev_lnw])
    aff_evs = []
    for c in range(NCH):
        e_a = p.op("vector", lambda v, c=c: v.tensor_scalar(
            xn2f[:, c, :], xn2f[:, c, :], cx.sC[:, 32 + c:33 + c], cx.sC[:, 40 + c:41 + c], ALU.mult, ALU.add),
            deps=[ev_lnb] + list(xn2_evs) + aff_evs[-1:])
        aff_evs.append(e_a)

    triu_bf = A3[:, 0:128]
    one_bf = A3[:, 128:256]
    ev_tr = p.dma(cx.ch_wg, lambda e: e.dma_start(out=triu_bf, in_=d["c_triu"]), deps=list(stat_evs))
    ev_on = p.dma(cx.ch_wg, lambda e: e.dma_start(out=one_bf, in_=d["c_one128"]), deps=list(stat_evs))
    ev_mk = p.dma(cx.ch_m, lambda e: e.dma_start(out=cx.msk_sb[:, 0:256], in_=d["c_msk"]), deps=[ev_lnb])
    m_str = cx.msk_sb[:, 0:128]
    m_inc = cx.msk_sb[:, 128:256]
    zeros128 = A3[:, 4480:4608]
    e_z = p.op("vector", lambda v: v.tensor_scalar(
        zeros128, cx.ones_f[:, 0:1].broadcast_to((PT, 128)), 0.0, None, ALU.mult), deps=[cx.ev_ones] + list(stat_evs))
    Z_S = A3[0:64, 256:1280]
    Z_R = A3[0:64, 1280:2304]
    e_zs = p.op("vector", lambda v: v.tensor_scalar(Z_S, cx.ones_f[0:64, 0:1].broadcast_to((64, 1024)), 0.0, None, ALU.mult), deps=[cx.ev_ones] + list(stat_evs))
    e_zr = p.op("vector", lambda v: v.tensor_scalar(Z_R, cx.ones_f[0:64, 0:1].broadcast_to((64, 1024)), 0.0, None, ALU.mult), deps=[e_zs])
    ev_ba = p.dma(cx.ch_m, lambda e: e.dma_start(out=cx.sC[:, 48:56], in_=d["l2_ba"]), deps=[ev_mk])
    ev_br = p.dma(cx.ch_m, lambda e: e.dma_start(out=cx.sC[:, 56:64], in_=d["l2_bar"]), deps=[ev_ba])
    e_nba = ev_ba
    e_nbr = ev_br
    ev_bb = p.dma(cx.ch_m, lambda e: e.dma_start(out=cx.sC[0:16, 64:65], in_=d["l2_bb"]), deps=[ev_br])
    ev_bg = p.dma(cx.ch_m, lambda e: e.dma_start(out=cx.sC[0:16, 65:66], in_=d["l2_bg"]), deps=[ev_bb])

    wbuf = A2[:, 0:8192].rearrange("p (c w) -> p c w", c=NCH)
    stg = A2[:, 8192:8704]
    scr_pos = {"q": (cx.scr_g, 0), "k": (cx.scr_g, 16384), "v": (cx.scr_g, 32768),
               "zA": (cx.scr_u, 0), "zR": (cx.scr_u, 16384)}
    stash_last = {}
    prev_q = []
    for qi, (qn, wn) in enumerate((("q", "l2_wq"), ("k", "l2_wk"), ("v", "l2_wv"), ("zA", "l2_wa"), ("zR", "l2_war"))):
        wevs = [p.dma(cx.ch_wg, lambda e, dc=dc, wn=wn: e.dma_start(
            out=wbuf[:, dc, :], in_=d[wn][dc * 128:(dc + 1) * 128, :]), deps=prev_q + aff_evs[-1:]) for dc in range(NCH)]
        nevs = []
        scr, base = scr_pos[qn]
        for fc in range(NCH):
            for tb in range(4):
                bank = cx.pb[(fc * 4 + tb) % 2]
                mm = None
                for dc in range(NCH):
                    war = [bw.get((fc * 4 + tb) % 2)] if dc == 0 else []
                    mm = p.op("tensor", lambda tt, dc=dc, fc=fc, tb=tb, bank=bank: tt.matmul(
                        bank[:, 0:512], wbuf[:, dc, fc * 128:(fc + 1) * 128], xn2f[:, dc, tb * 512:(tb + 1) * 512],
                        start=(dc == 0), stop=(dc == NCH - 1)),
                        deps=[wevs[dc]] + aff_evs[-1:] + war)
                if qi < 3:
                    e_e = p.op("vector", lambda v, bank=bank: v.tensor_copy(stg[:, 0:512], bank[:, 0:512]),
                               deps=[mm, stash_last.get("s")])
                else:
                    nb = cx.sC[:, 48 + 8 * (qi - 3) + fc:49 + 8 * (qi - 3) + fc]
                    e_e = p.op("vector", lambda v, bank=bank, nb=nb: v.tensor_scalar(stg[:, 0:512], bank[:, 0:512], nb[:, 0:1], None, ALU.add),
                               deps=[mm, stash_last.get("s"), e_nba, e_nbr])
                bw[(fc * 4 + tb) % 2] = e_e
                e_sd = p.dma(cx.ch_s, lambda e, scr=scr, base=base, fc=fc, tb=tb: e.dma_start(
                    out=scr[:, base + fc * 2048 + tb * 512:base + fc * 2048 + (tb + 1) * 512], in_=stg[:, 0:512]), deps=[e_e])
                stash_last["s"] = e_sd
        prev_q = [e_sd]
        stash_last[qn] = e_sd
    wbg = A2[:, 8704:8960].rearrange("p (c w) -> p c w", c=NCH)
    bg_evs = []
    for bi, (wn, bcol, rowb) in enumerate((("l2_wb", 64, 0), ("l2_wg", 65, 32))):
        wev = [p.dma(cx.ch_wg, lambda e, dc=dc, wn=wn: e.dma_start(
            out=wbg[:, dc, 0:16], in_=d[wn][dc * 128:(dc + 1) * 128, :]), deps=prev_q + bg_evs[-1:]) for dc in range(NCH)]
        for tb in range(4):
            bank = cx.pb[2]
            mm = None
            for dc in range(NCH):
                war = [bw.get(2)] if dc == 0 else []
                mm = p.op("tensor", lambda tt, dc=dc, tb=tb, bank=bank: tt.matmul(
                    bank[0:16, 0:512], wbg[:, dc, 0:16], xn2f[:, dc, tb * 512:(tb + 1) * 512],
                    start=(dc == 0), stop=(dc == NCH - 1)), deps=[wev[dc]] + aff_evs[-1:] + war)
            e_s = p.op("scalar", lambda s, tb=tb, bank=bank, bcol=bcol: s.activation(
                stg[0:16, 0:512], bank[0:16, 0:512], AF.Sigmoid,
                bias=cx.sC[0:16, bcol:bcol + 1]), deps=[mm, ev_bb, ev_bg] + bg_evs[-1:] + [stash_last.get("s")])
            bw[2] = e_s
            e_sd = p.dma(cx.ch_s, lambda e, tb=tb, bi=bi: e.dma_start(
                out=cx.scr_o[0:16, bi * 2048 + tb * 512:bi * 2048 + (tb + 1) * 512], in_=stg[0:16, 0:512]), deps=[e_s])
            bg_evs.append(e_sd)

    wo_sb = A3[:, 4608:12800].rearrange("p (c w) -> p c w", c=NCH)
    wo_evs = [p.dma(cx.ch_wg, lambda e, fc=fc: e.dma_start(
        out=wo_sb[:, fc, :], in_=d["l2_wo"][fc * 128:(fc + 1) * 128, :]), deps=prev_q + bg_evs[-1:]) for fc in range(NCH)]

    rl = {qn: A2[:, i * 1024:(i + 1) * 1024].rearrange("p (c w) -> p c w", c=NCH)
          for i, qn in enumerate(("q", "k", "v", "zA", "zR"))}
    q_t, kn_t, v_t = A2[:, 5120:6144], A2[:, 6144:7168], A2[:, 7168:8192]
    la_t, laR_t = A2[:, 8192:9216], A2[:, 9216:10240]
    o_t = A2[:, 10240:11264]
    d1 = A2[:, 11264:12288].bitcast(F32)
    d2 = A2[:, 12288:13312].bitcast(F32)
    la_loS = A1[:, 14336:15360]
    la_loR = A1[:, 15360:16384]
    pc_t, pcR_t = A2[:, 13312:14336], A2[:, 14336:15360]
    o_f = A2[:, 15360:16384].rearrange("p (c w) -> p c w", c=NCH)
    PR = {nm: A1[:, i * 1024:(i + 1) * 1024] for i, nm in enumerate(
        ("kgb", "kig", "khat", "ksh", "knAsh", "qg", "qp", "kcp",
         "kgg", "kigR", "khatR", "kshR", "knAshR", "qgR", "qpR", "kcpR"))}
    FQ = ("kgb", "kig", "khat", "ksh", "knAsh", "qg", "qp",
          "kgg", "kigR", "khatR", "kshR", "knAshR", "qgR", "qpR")
    ftile = {nm: A3[0:64, 2304 + i * 128:2304 + (i + 1) * 128] for i, nm in enumerate(FQ)}
    Wb_ = A3[:, 4096:4160]
    rbuf = A3[:, 4160:4224]
    Xb = A3[:, 4224:4352]
    XbT = A2[:, 15360:15488]
    Msel = A3[:, 4352:4480]
    bt_f32 = cx.sC[:, 80:96]
    gt_f32 = cx.sC[:, 96:112]
    nrmb = cx.sC[:, 112:128]
    scrC = cx.sC[:, 128:192]
    scrC2 = cx.sC[:, 192:256]
    xn_res = cx.sC[:, 256:1280]
    pcS_ap = cx.sC[0:64, 20:21]
    pcR_ap = cx.sC[0:64, 21:22]
    smbf = cx.small[:, 32:56].bitcast(BF16)
    bt_bf = smbf[:, 0:16]
    gt_bf = smbf[:, 16:32]
    rs_bf = smbf[:, 32:48]
    prev_n = [e_zr, e_z, ev_tr, ev_on, ev_mk] + bg_evs[-1:] + [stash_last["zR"]]
    x_upd = []
    for n in range(NC2):
        rl_evs = {}
        for i, qn in enumerate(("q", "k", "v", "zA", "zR")):
            scr, base = scr_pos[qn]
            rl_evs[qn] = p.dma(cx.ch_wg, lambda e, qn=qn, i=i, scr=scr, base=base, n=n: e.dma_start(
                out=rl[qn][:, :, :], in_=bass.AP(scr.tensor, base + n * 128, [[49152, 128], [2048, 8], [1, 128]])),
                deps=prev_n + [stash_last[qn]])
        tm = {}
        t_evs = {}
        for qn, dstb in (("q", q_t), ("k", kn_t), ("v", v_t), ("zA", la_t), ("zR", laR_t)):
            evs = []
            for fc in range(NCH):
                bank = cx.pbt[fc % 2]
                e_t = p.op("tensor", lambda tt, qn=qn, fc=fc, bank=bank: tt.transpose(
                    bank[:, 0:128], rl[qn][:, fc, :], cx.eye_bf[:, :]),
                    deps=[rl_evs[qn], cx.ev_eye, bw.get(("t", fc % 2))])
                e_c = p.op("vector", lambda v, dstb=dstb, fc=fc, bank=bank: v.tensor_copy(
                    dstb[:, fc * 128:(fc + 1) * 128], bank[:, 0:128]), deps=[e_t])
                bw[("t", fc % 2)] = e_c
                evs.append(e_c)
            t_evs[qn] = evs[-1]
        laf = cx.sC[:, 256:1280]
        e_m1 = p.op("scalar", lambda s: s.activation(cx.sB[:, 0:1024], la_t, AF.Sigmoid), deps=[t_evs["zA"]] + prev_n[:2])
        e_f1 = p.op("scalar", lambda s: s.activation(laf, cx.sB[:, 0:1024], AF.Ln), deps=[e_m1])
        e_h1 = p.op("vector", lambda v: v.tensor_copy(la_t, laf), deps=[e_f1])
        e_l1 = p.op("vector", lambda v: v.tensor_tensor(la_loS, laf, la_t, ALU.subtract), deps=[e_h1])
        e_m2 = p.op("scalar", lambda s: s.activation(cx.sB[:, 0:1024], laR_t, AF.Sigmoid), deps=[t_evs["zR"], e_l1])
        e_f2 = p.op("scalar", lambda s: s.activation(laf, cx.sB[:, 0:1024], AF.Ln), deps=[e_m2])
        e_h2 = p.op("vector", lambda v: v.tensor_copy(laR_t, laf), deps=[e_f2])
        e_l2 = p.op("vector", lambda v: v.tensor_tensor(la_loR, laf, laR_t, ALU.subtract), deps=[e_h2])
        e_k1 = p.op("scalar", lambda s: s.activation(cx.sB[:, 0:1024], kn_t, AF.Square), deps=[t_evs["k"], e_l2])
        e_k2 = p.op("vector", lambda v: v.tensor_reduce(nrmb, cx.sB[:, 0:1024].rearrange("p (h e) -> p h e", h=16), axis=AX.X, op=ALU.add), deps=[e_k1])
        e_k3 = p.op("scalar", lambda s: s.activation(nrmb, nrmb, AF.Sqrt), deps=[e_k2])
        e_k4 = p.op("vector", lambda v: v.tensor_scalar(nrmb, nrmb, 1e-12, None, ALU.max), deps=[e_k3])
        e_k5 = p.op("vector", lambda v: v.reciprocal(nrmb, nrmb), deps=[e_k4])
        e_k6 = p.op("vector", lambda v: v.tensor_copy(rs_bf, nrmb), deps=[e_k5])
        e_k7 = p.op("vector", lambda v: v.tensor_tensor(
            kn_t.rearrange("p (h e) -> p h e", h=16), kn_t.rearrange("p (h e) -> p h e", h=16),
            rs_bf.rearrange("p h -> p h 1" if False else "p (h o) -> p h o", o=1).broadcast_to((PT, 16, 64)), ALU.mult), deps=[e_k6])
        ev_bgl = p.dma(cx.ch_m, lambda e, n=n: e.dma_start(
            out=cx.rpb_sb[0:16, 0:128], in_=cx.scr_o[0:16, n * 128:(n + 1) * 128]),
            deps=bg_evs[-1:] + prev_n + [bw.get("bgl")])
        ev_bgl2 = p.dma(cx.ch_m, lambda e, n=n: e.dma_start(
            out=cx.rpb_sb[0:16, 128:256], in_=cx.scr_o[0:16, 2048 + n * 128:2048 + (n + 1) * 128]),
            deps=[ev_bgl])
        e_b1 = p.op("tensor", lambda tt: tt.transpose(
            cx.pbt[0][:, 0:16], cx.rpb_sb[0:16, 0:128], cx.eye_bf[0:16, 0:16]),
            deps=[ev_bgl, cx.ev_eye, bw.get(("t", 0))])
        e_b2a = p.op("vector", lambda v: v.tensor_copy(bt_f32, cx.pbt[0][:, 0:16]), deps=[e_b1])
        e_b2 = p.op("vector", lambda v: v.tensor_copy(bt_bf, cx.pbt[0][:, 0:16]), deps=[e_b1, e_b2a])
        e_b3 = p.op("tensor", lambda tt: tt.transpose(
            cx.pbt[0][:, 0:16], cx.rpb_sb[0:16, 128:256], cx.eye_bf[0:16, 0:16]),
            deps=[ev_bgl2, cx.ev_eye, e_b2])
        e_b4a = p.op("vector", lambda v: v.tensor_copy(gt_f32, cx.pbt[0][:, 0:16]), deps=[e_b3])
        e_b4 = p.op("vector", lambda v: v.tensor_copy(gt_bf, cx.pbt[0][:, 0:16]), deps=[e_b3, e_b4a])
        bw[("t", 0)] = e_b4
        bw["bgl"] = e_b3
        side_prod_evs = []
        for side in range(2):
            la_b = la_t if side == 0 else laR_t
            e_la = e_l1 if side == 0 else e_l2
            la_lo = la_loS if side == 0 else la_loR
            lam_mm = []
            for half in range(2):
                bkL = cx.pb[0 + half]
                bkT = cx.pb[2 + half]
                m_a0 = p.op("tensor", lambda tt, half=half, la_b=la_b, bkL=bkL: tt.matmul(
                    bkL[:, 0:512], triu_bf, la_b[:, half * 512:(half + 1) * 512], start=True, stop=False),
                    deps=[e_la, ev_tr, bw.get(half)])
                m_a = p.op("tensor", lambda tt, half=half, la_lo=la_lo, bkL=bkL: tt.matmul(
                    bkL[:, 0:512], triu_bf, la_lo[:, half * 512:(half + 1) * 512], start=False, stop=True),
                    deps=[m_a0])
                m_b0 = p.op("tensor", lambda tt, half=half, la_b=la_b, bkT=bkT: tt.matmul(
                    bkT[:, 0:512], one_bf, la_b[:, half * 512:(half + 1) * 512], start=True, stop=False),
                    deps=[e_la, ev_on, bw.get(2 + half)])
                m_b = p.op("tensor", lambda tt, half=half, la_lo=la_lo, bkT=bkT: tt.matmul(
                    bkT[:, 0:512], one_bf, la_lo[:, half * 512:(half + 1) * 512], start=False, stop=True),
                    deps=[m_b0])
                lam_mm.append((m_a, m_b))
            pcX = pc_t if side == 0 else pcR_t
            pdep = [e_k7] + side_prod_evs[-1:]
            evs = []
            for half in range(2):
                sl = slice(half * 512, (half + 1) * 512)
                L = cx.pb[0 + half][:, 0:512]
                Lt = cx.pb[2 + half][:, 0:512]
                sb1 = cx.sB[:, 0:512]
                sb2 = cx.sB[:, 512:1024]
                dd = [lam_mm[half][0], lam_mm[half][1]] + pdep + evs[-1:]
                e1 = p.op("vector", lambda v, Lt=Lt, sb1=sb1: v.tensor_scalar(sb1, Lt, -0.5, None, ALU.mult), deps=dd)
                e2 = p.op("vector", lambda v, L=L, sb1=sb1: v.tensor_tensor(sb1, sb1, L, ALU.add), deps=[e1])
                e3 = p.op("vector", lambda v, sb1=sb1: v.tensor_scalar(sb1, sb1, 60.0, -60.0, ALU.min, ALU.max), deps=[e2])
                e4 = p.op("scalar", lambda s, sb1=sb1: s.activation(d1[:, 0:512], sb1, AF.Exp), deps=[e3])
                e5 = p.op("vector", lambda v, sb1=sb1, sb2=sb2: v.tensor_scalar(sb2, sb1, -1.0, None, ALU.mult), deps=[e4])
                e6 = p.op("scalar", lambda s, sb2=sb2: s.activation(d2[:, 0:512], sb2, AF.Exp), deps=[e5])
                nm_g = "kgb" if side == 0 else "kgg"
                nm_gi = "kig" if side == 0 else "kigR"
                nm_qg = "qg" if side == 0 else "qgR"
                btx = bt_bf if side == 0 else gt_bf
                btxf = bt_f32 if side == 0 else gt_f32
                e7 = p.op("vector", lambda v, sl=sl: v.tensor_tensor(
                    cx.sA[:, 0:512], kn_t[:, sl], d1[:, 0:512], ALU.mult), deps=[e6, e_k7])
                e8 = p.op("vector", lambda v, sl=sl, nm_g=nm_g, btxf=btxf, half=half: v.tensor_tensor(
                    PR[nm_g][:, sl].rearrange("p (h e) -> p h e", h=8), cx.sA[:, 0:512].rearrange("p (h e) -> p h e", h=8),
                    btxf[:, half * 8:(half + 1) * 8].rearrange("p (h o) -> p h o", o=1).broadcast_to((PT, 8, 64)), ALU.mult), deps=[e7, e_b4])
                e9 = p.op("vector", lambda v, sl=sl, nm_gi=nm_gi: v.tensor_tensor(PR[nm_gi][:, sl], kn_t[:, sl], d2[:, 0:512], ALU.mult), deps=[e8])
                e10 = p.op("vector", lambda v, sl=sl, nm_qg=nm_qg: v.tensor_tensor(PR[nm_qg][:, sl], q_t[:, sl], d1[:, 0:512], ALU.mult), deps=[e9, t_evs["q"]])
                nm_ks = "ksh" if side == 0 else "kshR"
                e11 = p.op("vector", lambda v, sb1=sb1, sl=sl, la_b=la_b: v.tensor_tensor(sb1, sb1, la_b[:, sl], ALU.subtract), deps=[e10])
                e11b = p.op("vector", lambda v, sb1=sb1: v.tensor_scalar(sb1, sb1, 60.0, -60.0, ALU.min, ALU.max), deps=[e11])
                e12 = p.op("scalar", lambda s, sb1=sb1: s.activation(d1[:, 0:512], sb1, AF.Exp), deps=[e11b])
                e13 = p.op("vector", lambda v, sl=sl, nm_ks=nm_ks: v.tensor_tensor(PR[nm_ks][:, sl], kn_t[:, sl], d1[:, 0:512], ALU.mult), deps=[e12])
                nm_kh = "khat" if side == 0 else "khatR"
                nm_qp = "qp" if side == 0 else "qpR"
                nm_kA = "knAsh" if side == 0 else "knAshR"
                nm_kc = "kcp" if side == 0 else "kcpR"
                e14 = p.op("scalar", lambda s, L=L: s.activation(d1[:, 0:512], L, AF.Exp), deps=[e13])
                e15 = p.op("vector", lambda v, sl=sl, nm_kh=nm_kh: v.tensor_tensor(PR[nm_kh][:, sl], kn_t[:, sl], d1[:, 0:512], ALU.mult), deps=[e14])
                e16 = p.op("vector", lambda v, sl=sl, nm_qp=nm_qp: v.tensor_tensor(PR[nm_qp][:, sl], q_t[:, sl], d1[:, 0:512], ALU.mult), deps=[e15])
                e17 = p.op("vector", lambda v, L=L, sb1=sb1, sl=sl, la_b=la_b: v.tensor_tensor(sb1, L, la_b[:, sl], ALU.subtract), deps=[e16])
                e18 = p.op("scalar", lambda s, sb1=sb1: s.activation(d1[:, 0:512], sb1, AF.Exp), deps=[e17])
                e19 = p.op("vector", lambda v, sl=sl, nm_kA=nm_kA: v.tensor_tensor(PR[nm_kA][:, sl], kn_t[:, sl], d1[:, 0:512], ALU.mult), deps=[e18])
                e20a = p.op("vector", lambda v, Lt=Lt, sb2=sb2: v.tensor_scalar(sb2, Lt, 1.0, None, ALU.mult), deps=[e19])
                e20 = p.op("vector", lambda v, L=L, sb1=sb1, sb2=sb2: v.tensor_tensor(sb1, sb2, L, ALU.subtract), deps=[e20a])
                e21 = p.op("scalar", lambda s, sb1=sb1: s.activation(d1[:, 0:512], sb1, AF.Exp), deps=[e20])
                e22 = p.op("vector", lambda v, sl=sl, nm_kc=nm_kc: v.tensor_tensor(PR[nm_kc][:, sl], kn_t[:, sl], d1[:, 0:512], ALU.mult), deps=[e21])
                e23 = p.op("scalar", lambda s, Lt=Lt, pcX=pcX, sl=sl: s.activation(pcX[:, sl], Lt, AF.Exp), deps=[e22])
                evs.append(e23)
            bw[0], bw[1] = evs[-2] if len(evs) > 1 else evs[-1], evs[-1]
            bw[2], bw[3] = bw[0], bw[1]
            side_prod_evs.append(evs[-1])
        prod_done = side_prod_evs[-1]
        rstd_ap = cx.ln_stats[:, 2 * n:2 * n + 1]
        nmr_ap = cx.ln_stats[:, 2 * n + 1:2 * n + 2]
        e_xr = p.op("vector", lambda v, n=n, rstd_ap=rstd_ap, nmr_ap=nmr_ap: v.tensor_scalar(
            xn_res, cx.x_sb[:, n, :], rstd_ap[:, 0:1], nmr_ap[:, 0:1], ALU.mult, ALU.add),
            deps=[prod_done] + list(stat_evs))

        o_evs = []
        st_evs = []
        for h in range(16):
            hs = slice(h * 64, (h + 1) * 64)
            f_evs = {}
            pv = None
            for qi, nm in enumerate(FQ):
                bank = cx.pbt[qi % 2]
                e_t = p.op("tensor", lambda tt, nm=nm, hs=hs, bank=bank: tt.transpose(
                    bank[0:64, 0:128], PR[nm][:, hs], cx.eye_bf[:, :]),
                    deps=[prod_done, cx.ev_eye, bw.get(("t", qi % 2))])
                e_c = p.op("vector", lambda v, nm=nm, bank=bank: v.tensor_copy(ftile[nm], bank[0:64, 0:128]), deps=[e_t, pv])
                bw[("t", qi % 2)] = e_c
                f_evs[nm] = e_c
                pv = e_c
            e_p1 = p.op("tensor", lambda tt, h=h: tt.transpose(
                cx.pbt[0][0:64, 0:1], pc_t[0:1, h * 64:(h + 1) * 64], cx.eye_bf[0:1, 0:1]),
                deps=[prod_done, bw.get(("t", 0)), cx.ev_eye])
            e_p2 = p.op("vector", lambda v: v.tensor_copy(pcS_ap, cx.pbt[0][0:64, 0:1]), deps=[e_p1])
            e_p3 = p.op("tensor", lambda tt, h=h: tt.transpose(
                cx.pbt[0][0:64, 0:1], pcR_t[0:1, h * 64:(h + 1) * 64], cx.eye_bf[0:1, 0:1]),
                deps=[prod_done, e_p2, cx.ev_eye])
            e_p4 = p.op("vector", lambda v: v.tensor_copy(pcR_ap, cx.pbt[0][0:64, 0:1]), deps=[e_p3])
            bw[("t", 0)] = e_p4

            def solve_W(knm_g, knm_gi, khat_nm, Zb, val_ap, val_dep, sc_ap, e_pref):
                mmX = p.op("tensor", lambda tt: tt.matmul(
                    cx.pb[4][:, 0:128], ftile[knm_gi], ftile[knm_g], start=True, stop=True),
                    deps=[f_evs[knm_gi], f_evs[knm_g], bw.get(4)] + e_pref)
                eX = p.op("vector", lambda v: v.select(Xb, m_str, cx.pb[4][:, 0:128], zeros128), deps=[mmX, ev_mk, e_z, bw.get("Xb")])
                mmZ = p.op("tensor", lambda tt: tt.matmul(
                    cx.pb[5][:, 0:64], ftile[khat_nm], Zb, start=True, stop=True),
                    deps=[f_evs[khat_nm], bw.get(5)] + st_evs[-2:])
                eU1 = p.op("vector", lambda v: v.tensor_tensor(scrC, val_ap, cx.pb[5][:, 0:64], ALU.subtract), deps=[mmZ] + val_dep)
                eU2 = p.op("vector", lambda v: v.tensor_scalar(Wb_, scrC, sc_ap[:, 0:1], None, ALU.mult), deps=[eU1, e_b4, bw.get("Wb")])
                bw[5] = eU2
                last = eU2
                lastX = eX
                if getattr(cx, "_l2hook", None) is not None and knm_g == "kgb":
                    dW = cx._l2hook
                    ev1 = p.dma(cx.ch_m, lambda e: e.dma_start(out=dW[:, 0:128], in_=Xb), deps=[eX])
                    ev2 = p.dma(cx.ch_m, lambda e: e.dma_start(out=dW[:, 128:192], in_=Wb_), deps=[eU2, ev1])
                    cx.dbg_evs.extend([ev1, ev2])
                    last = ev2
                    lastX = ev1
                for lv in range(7):
                    mmA = p.op("tensor", lambda tt: tt.matmul(
                        cx.pb[4][:, 128:192], Xb, Wb_, start=True, stop=True), deps=[lastX, last, bw.get(4)])
                    eW = p.op("vector", lambda v, lv=lv: v.tensor_tensor(
                        Wb_, Wb_, cx.pb[4][:, 128:192], ALU.subtract if lv == 0 else ALU.add), deps=[mmA])
                    last = eW
                    if lv < 6:
                        e_tx = p.op("tensor", lambda tt: tt.transpose(
                            cx.pbt[1][:, 0:128], Xb, cx.eye_bf[:, :]), deps=[lastX, mmA, cx.ev_eye, bw.get(("t", 1))])
                        e_cx = p.op("vector", lambda v: v.tensor_copy(XbT, cx.pbt[1][:, 0:128]), deps=[e_tx, bw.get("XbT")])
                        bw[("t", 1)] = e_cx
                        mmB = p.op("tensor", lambda tt: tt.matmul(
                            cx.pb[4][:, 256:384], XbT, Xb, start=True, stop=True), deps=[e_cx])
                        bw["XbT"] = mmB
                        eX2 = p.op("vector", lambda v: v.tensor_copy(Xb, cx.pb[4][:, 256:384]), deps=[mmB, eW])
                        lastX = eX2
                        bw[4] = eX2
                    else:
                        bw[4] = eW
                if getattr(cx, "_l2hook", None) is not None and knm_g == "kgb":
                    dW = cx._l2hook
                    ev3 = p.dma(cx.ch_m, lambda e: e.dma_start(out=dW[:, 192:256], in_=Wb_), deps=[last])
                    cx.dbg_evs.append(ev3)
                    last = ev3
                bw["Xb"] = lastX
                bw["Wb"] = last
                return last

            if "l2w" in DBG and n == 0 and h == 0:
                cx._l2hook = cx.nc.dram_tensor("y_l2w", [PT, 256], BF16, kind="ExternalOutput").ap()
            else:
                cx._l2hook = None
            bS = bt_f32[:, h:h + 1]
            eWS = solve_W("kgb", "kig", "khat", Z_S[0:64, hs], v_t[:, hs], [t_evs["v"]], bS, [])
            mmP1 = p.op("tensor", lambda tt: tt.matmul(
                cx.pb[4][:, 384:512], ftile["kig"], ftile["ksh"], start=True, stop=True),
                deps=[f_evs["kig"], f_evs["ksh"], bw.get(4)])
            eP = p.op("vector", lambda v: v.select(Msel, m_str, cx.pb[4][:, 384:512], zeros128), deps=[mmP1, ev_mk, e_z, bw.get("Msel")])
            bw[4] = eP
            mmP2 = p.op("tensor", lambda tt: tt.matmul(
                cx.pb[5][:, 64:128], Msel, Wb_, start=True, stop=False), deps=[eP, eWS, bw.get(5)])
            mmP3 = p.op("tensor", lambda tt, hs=hs: tt.matmul(
                cx.pb[5][:, 64:128], ftile["knAsh"], Z_S[0:64, hs], start=False, stop=True),
                deps=[f_evs["knAsh"], mmP2] + st_evs[-2:])
            eR1 = p.op("vector", lambda v, hs=hs: v.tensor_tensor(scrC2, v_t[:, hs], cx.pb[5][:, 64:128], ALU.subtract), deps=[mmP3, t_evs["v"]])
            eR2 = p.op("vector", lambda v: v.tensor_scalar(rbuf, scrC2, 1.0, -1.0, ALU.min, ALU.max), deps=[eR1, bw.get("rbuf")])
            bw[5] = eR2
            mmO1 = p.op("tensor", lambda tt, hs=hs: tt.matmul(
                cx.pb[3][:, 0:64], ftile["qp"], Z_S[0:64, hs], start=True, stop=False),
                deps=[f_evs["qp"], bw.get(3)] + st_evs[-2:])
            mmQK = p.op("tensor", lambda tt: tt.matmul(
                cx.pb[4][:, 384:512], ftile["kig"], ftile["qg"], start=True, stop=True),
                deps=[f_evs["kig"], f_evs["qg"], eP, bw.get(4), eR2])
            eQs = p.op("vector", lambda v: v.select(Msel, m_inc, cx.pb[4][:, 384:512], zeros128), deps=[mmQK, ev_mk, e_z, mmP2, mmO1])
            bw[4] = eQs
            bw["Msel"] = eQs
            mmO2 = p.op("tensor", lambda tt: tt.matmul(
                cx.pb[3][:, 0:64], Msel, Wb_, start=False, stop=False), deps=[eQs, eWS, mmO1])
            mmS = p.op("tensor", lambda tt, hs=hs: tt.matmul(
                cx.pb[2][0:64, 0:64], PR["kcp"][:, hs], Wb_, start=True, stop=True),
                deps=[prod_done, eWS, bw.get(2), mmO2])
            eS1 = p.op("vector", lambda v, hs=hs: v.tensor_scalar(scrC[0:64, 0:64], Z_S[0:64, hs], pcS_ap[0:64, 0:1], None, ALU.mult), deps=[mmS, e_p2])
            eS2 = p.op("vector", lambda v, hs=hs: v.tensor_tensor(Z_S[0:64, hs], scrC[0:64, 0:64], cx.pb[2][0:64, 0:64], ALU.add), deps=[eS1])
            bw[2] = eS2
            st_evs.append(eS2)
            gS = gt_f32[:, h:h + 1]
            eWR = solve_W("kgg", "kigR", "khatR", Z_R[0:64, hs], rbuf, [eR2], gS, [eQs])
            mmO3 = p.op("tensor", lambda tt, hs=hs: tt.matmul(
                cx.pb[3][:, 0:64], ftile["qpR"], Z_R[0:64, hs], start=False, stop=False),
                deps=[f_evs["qpR"], mmO2] + st_evs[-2:])
            mmQKR = p.op("tensor", lambda tt: tt.matmul(
                cx.pb[4][:, 384:512], ftile["kigR"], ftile["qgR"], start=True, stop=True),
                deps=[f_evs["kigR"], f_evs["qgR"], bw.get(4), eWR])
            eQsR = p.op("vector", lambda v: v.select(Msel, m_inc, cx.pb[4][:, 384:512], zeros128), deps=[mmQKR, ev_mk, e_z, mmO3])
            bw[4] = eQsR
            bw["Msel"] = eQsR
            mmO4 = p.op("tensor", lambda tt: tt.matmul(
                cx.pb[3][:, 0:64], Msel, Wb_, start=False, stop=True), deps=[eQsR, eWR, mmO3])
            eO = p.op("vector", lambda v, hs=hs: v.tensor_copy(o_t[:, hs], cx.pb[3][:, 0:64]), deps=[mmO4, bw.get("o_t")])
            bw[3] = eO
            o_evs.append(eO)
            mmR = p.op("tensor", lambda tt, hs=hs: tt.matmul(
                cx.pb[2][0:64, 0:64], PR["kcpR"][:, hs], Wb_, start=True, stop=True),
                deps=[prod_done, eWR, bw.get(2), mmO4])
            eR3 = p.op("vector", lambda v, hs=hs: v.tensor_scalar(scrC[0:64, 0:64], Z_R[0:64, hs], pcR_ap[0:64, 0:1], None, ALU.mult), deps=[mmR, e_p4])
            eR4 = p.op("vector", lambda v, hs=hs: v.tensor_tensor(Z_R[0:64, hs], scrC[0:64, 0:64], cx.pb[2][0:64, 0:64], ALU.add), deps=[eR3])
            bw[2] = eR4
            st_evs.append(eR4)
        if "l2d" in DBG and n == 0:
            dbg2 = cx.nc.dram_tensor("y_l2bg", [PT, 1024], BF16, kind="ExternalOutput").ap()
            cx.dbg_evs.append(p.dma(cx.ch_m, lambda e: e.dma_start(out=dbg2[0:16, 0:256], in_=cx.rpb_sb[0:16, 0:256]), deps=o_evs))
            cx.dbg_evs.append(p.dma(cx.ch_m, lambda e: e.dma_start(out=dbg2[:, 256:272], in_=bt_bf), deps=o_evs + cx.dbg_evs[-1:]))
            cx.dbg_evs.append(p.dma(cx.ch_m, lambda e: e.dma_start(out=dbg2[0:16, 512:1024], in_=cx.scr_o[0:16, 0:512]), deps=o_evs + cx.dbg_evs[-1:]))
            dL = cx.nc.dram_tensor("y_l2n0", [PT, 8192], BF16, kind="ExternalOutput").ap()
            dz = cx.nc.dram_tensor("y_l2zs", [64, 2048], BF16, kind="ExternalOutput").ap()
            dmp = []
            for i, buf in enumerate((q_t, kn_t, v_t, la_t, pc_t, o_t, PR["kgb"], PR["kig"])):
                dmp.append(p.dma(cx.ch_m, lambda e, i=i, buf=buf: e.dma_start(
                    out=dL[:, i * 1024:(i + 1) * 1024], in_=buf), deps=o_evs + st_evs + dmp[-1:]))
            dmp.append(p.dma(cx.ch_m, lambda e: e.dma_start(out=dz[:, 0:1024], in_=Z_S), deps=st_evs + dmp[-1:]))
            dmp.append(p.dma(cx.ch_m, lambda e: e.dma_start(out=dz[:, 1024:2048], in_=Z_R), deps=st_evs + dmp[-1:]))
            cx.dbg_evs.extend(dmp)
            prev_n = prev_n + dmp[-1:]
            o_evs = o_evs + dmp[-1:]
        if "l2o" in DBG:
            if n == 0:
                cx._l2o = cx.nc.dram_tensor("y_l2o", [PT, 16384], BF16, kind="ExternalOutput").ap()
            ev_do = p.dma(cx.ch_m, lambda e, n=n: e.dma_start(out=cx._l2o[:, n * 1024:(n + 1) * 1024], in_=o_t), deps=o_evs + [bw.get("l2od")])
            bw["l2od"] = ev_do
            cx.dbg_evs.append(ev_do)
            o_evs = o_evs + [ev_do]
        of_evs = []
        for fc in range(NCH):
            bank = cx.pbt[fc % 2]
            e_t = p.op("tensor", lambda tt, fc=fc, bank=bank: tt.transpose(
                bank[:, 0:128], o_t[:, fc * 128:(fc + 1) * 128], cx.eye_bf[:, :]),
                deps=o_evs + [cx.ev_eye, bw.get(("t", fc % 2))])
            e_c = p.op("vector", lambda v, fc=fc, bank=bank: v.tensor_copy(o_f[:, fc, :], bank[:, 0:128]),
                       deps=[e_t, bw.get(("of", fc))])
            bw[("t", fc % 2)] = e_c
            of_evs.append(e_c)
        mm_l = [None, None]
        for fc in range(NCH):
            for half in range(2):
                bank = cx.pb[0 + half]
                war = [bw.get(0 + half)] if fc == 0 else []
                mm = p.op("tensor", lambda tt, fc=fc, half=half, bank=bank: tt.matmul(
                    bank[:, 0:512], o_f[:, fc, :], wo_sb[:, fc, half * 512:(half + 1) * 512],
                    start=(fc == 0), stop=(fc == NCH - 1)), deps=[of_evs[fc], wo_evs[fc]] + war + ([mm_l[half]] if mm_l[half] else []))
                mm_l[half] = mm
            bw[("of", fc)] = mm_l[1]
        nx = []
        for half in range(2):
            e_x1 = p.op("vector", lambda v, n=n, half=half: v.tensor_tensor(
                cx.x_sb[:, n, half * 512:(half + 1) * 512], cx.x_sb[:, n, half * 512:(half + 1) * 512],
                cx.pb[0 + half][:, 0:512], ALU.add), deps=[mm_l[half], e_xr])
            e_x2 = p.op("vector", lambda v, n=n, half=half: v.tensor_tensor(
                cx.x_sb[:, n, half * 512:(half + 1) * 512], cx.x_sb[:, n, half * 512:(half + 1) * 512],
                xn_res[:, half * 512:(half + 1) * 512], ALU.add), deps=[e_x1])
            bw[0 + half] = e_x2
            nx.append(e_x2)
        x_upd.extend(nx)
        prev_n = nx + [st_evs[-1], st_evs[-2]] + of_evs[-1:] + [e_xr]
    if "nol2ffn" in STAGES:
        return x_upd
    x_upd = emit_l2ffn(cx, stat_evs, xnf_evs, x_upd)
    return x_upd


def emit_l2ffn(cx, stat_evs, xnf_evs, x_upd):
    p, d = cx.p, cx.dram
    A1, A2, A3 = cx.AR1, cx.AR2, cx.AR3
    bw = cx.bw
    ce = cx.ceff
    ev_b1 = p.dma(cx.ch_m, lambda e: e.dma_start(out=cx.sC[:, 32:64], in_=d["l2_b1"]), deps=x_upd[-2:])
    e_b1s = p.op("vector", lambda v: v.tensor_scalar(cx.sC[:, 32:64], cx.sC[:, 32:64], 256.0, None, ALU.mult), deps=[ev_b1])
    ones_r = A3[:, 128:256]
    e_or = p.op("vector", lambda v: v.tensor_scalar(ones_r[0:1, 0:128], cx.ones_f[0:1, 0:1].broadcast_to((1, 128)), 1.0, None, ALU.mult), deps=[cx.ev_ones] + x_upd[-2:])
    b2bf = A3[0:1, 256:1280]
    ev_b2 = p.dma(cx.ch_wg, lambda e: e.dma_start(out=b2bf, in_=d["l2_b2"]), deps=x_upd[-2:])
    b2_rep = cx.sC[:, 256:1280]
    rep_evs = []
    for half in range(2):
        bank = cx.pb[2]
        em = p.op("tensor", lambda tt, half=half, bank=bank: tt.matmul(
            bank[:, 0:512], ones_r[0:1, 0:128], b2bf[0:1, half * 512:(half + 1) * 512],
            start=True, stop=True), deps=[ev_b2, e_or, bw.get(2)] + rep_evs[-1:])
        ec = p.op("vector", lambda v, half=half, bank=bank: v.tensor_copy(b2_rep[:, half * 512:(half + 1) * 512], bank[:, 0:512]), deps=[em])
        bw[2] = ec
        rep_evs.append(ec)
    A3f32 = A3[:, 4608:12800].bitcast(F32)
    zr512 = A3f32[:, 0:512]
    fv = A3f32[:, 512:1024]
    g0b = A3f32[:, 1024:1536]
    fA = cx.sA[:, 0:512]
    fB = cx.sA[:, 512:1024]
    fC = cx.sB[:, 512:1024]
    mk1 = cx.msk_sb[:, 256:768]
    e_z5 = p.op("vector", lambda v: v.tensor_scalar(zr512, cx.ones_f[:, 0:1].broadcast_to((PT, 512)), 0.0, None, ALU.mult), deps=[cx.ev_ones] + x_upd[-2:])
    w1b = A2[:, 0:8192].rearrange("p (c a w) -> p c a w", c=NCH, a=NCH)
    h_f = A1[:, 0:16384].rearrange("p (c w) -> p c w", c=32)
    w2b = A2[:, 8192:9216]
    x2_upd = []
    prev_w1 = []
    for tb in range(4):
        h_evs = []
        for fc in range(32):
            ev_w1 = p.dma(cx.ch_wg, lambda e, fc=fc: e.dma_start(
                out=w1b[:, fc % 8, :, :], in_=bass.AP(d["l2_w1"].tensor, fc * 128, [[4096, 128], [524288, 8], [1, 128]])),
                deps=prev_w1 + x_upd[-2:] + h_evs[-1:] + [bw.get(("w1", fc % 8))])
            bank = cx.pb[4 + fc % 2]
            mm = None
            for dc in range(NCH):
                war = [bw.get(4 + fc % 2)] if dc == 0 else []
                mm = p.op("tensor", lambda tt, dc=dc, fc=fc, tb=tb, bank=bank: tt.matmul(
                    bank[:, 0:512], w1b[:, fc % 8, dc, :], cx.xnf[:, dc, tb * 512:(tb + 1) * 512],
                    start=(dc == 0), stop=(dc == NCH - 1)), deps=[ev_w1] + list(xnf_evs) + war)
            bw[("w1", fc % 8)] = mm
            b1_ap = cx.sC[:, 32 + fc:33 + fc]
            dd = [mm, e_b1s, cx.ev_ceff, e_z5] + h_evs[-1:]
            q1 = p.op("vector", lambda v, bank=bank, b1_ap=b1_ap: v.tensor_scalar(fA, bank[:, 0:512], 256.0, b1_ap[:, 0:1], ALU.mult, ALU.add), deps=dd)
            q2 = p.op("vector", lambda v: v.tensor_scalar(fA, fA, ce[:, 0:1], None, ALU.add), deps=[q1])
            q3 = p.op("vector", lambda v: v.tensor_scalar(fA, fA, ce[:, 0:1], 1.0 / 256.0, ALU.subtract, ALU.mult), deps=[q2])
            bw[4 + fc % 2] = q1
            v1 = p.op("vector", lambda v: v.tensor_scalar(fB, fA, 4.0, None, ALU.mult), deps=[q3])
            v2 = p.op("vector", lambda v: v.tensor_scalar(fB, fB, 16.0, -16.0, ALU.min, ALU.max), deps=[v1])
            f0 = p.op("vector", lambda v: v.tensor_scalar(fv, fB, -0.4921875, None, ALU.add), deps=[v2])
            f1 = p.op("vector", lambda v: v.tensor_scalar(fv, fv, ce[:, 0:1], None, ALU.add), deps=[f0])
            f2 = p.op("vector", lambda v: v.tensor_scalar(fv, fv, ce[:, 0:1], None, ALU.subtract), deps=[f1])
            f4 = p.op("vector", lambda v: v.tensor_tensor(fB, fB, fv, ALU.subtract), deps=[f2])
            g1 = p.op("vector", lambda v: v.tensor_scalar(fC, fv, 0.25, None, ALU.mult), deps=[f4])
            g2 = p.op("scalar", lambda s: s.activation(g0b, fC, AF.Gelu), deps=[g1])
            g3 = p.op("vector", lambda v: v.tensor_scalar(fC, fC, 0.25, None, ALU.add), deps=[g2])
            g4 = p.op("scalar", lambda s: s.activation(fv, fC, AF.Gelu), deps=[g3])
            g5 = p.op("vector", lambda v: v.tensor_tensor(fv, fv, g0b, ALU.subtract), deps=[g4])
            g6 = p.op("vector", lambda v: v.tensor_tensor(fv, fv, fB, ALU.mult), deps=[g5])
            g7 = p.op("vector", lambda v: v.tensor_tensor(fv, fv, g0b, ALU.add), deps=[g6])
            m1 = p.op("vector", lambda v: v.tensor_scalar(mk1, fA, 4.0, None, ALU.is_gt), deps=[g7])
            s1_ = p.op("vector", lambda v: v.select(fv, mk1, fA, fv), deps=[m1])
            m2 = p.op("vector", lambda v: v.tensor_scalar(fB, fA, -1.0, None, ALU.mult), deps=[s1_])
            m3 = p.op("vector", lambda v: v.tensor_scalar(mk1, fB, 4.0, None, ALU.is_gt), deps=[m2])
            s2_ = p.op("vector", lambda v: v.select(fv, mk1, zr512, fv), deps=[m3, e_z5])
            o1 = p.op("vector", lambda v: v.tensor_scalar(fv, fv, 256.0, ce[:, 0:1], ALU.mult, ALU.add), deps=[s2_])
            o2 = p.op("vector", lambda v, fc=fc: v.tensor_scalar(
                h_f[:, fc, 0:512], fv, ce[:, 0:1], 1.0 / 256.0, ALU.subtract, ALU.mult), deps=[o1, bw.get(("hf", fc))])
            h_evs.append(o2)
        sub_evs = []
        for sp in range(2):
            for fc in range(32):
                ev_w2 = p.dma(cx.ch_wg, lambda e, fc=fc: e.dma_start(
                    out=w2b[:, 0:1024], in_=d["l2_w2"][fc * 128:(fc + 1) * 128, :]),
                    deps=[h_evs[fc]] + sub_evs[-1:] + [bw.get("w2l")])
                mm_last = None
                for si in range(2):
                    sub = sp * 2 + si
                    for half in range(2):
                        bank = cx.pb[2 * si + half]
                        war = [bw.get(("o2", sub, half))] if fc == 0 else []
                        mm = p.op("tensor", lambda tt, fc=fc, sub=sub, half=half, bank=bank: tt.matmul(
                            bank[:, 0:512], h_f[:, fc, sub * 128:(sub + 1) * 128], w2b[:, half * 512:(half + 1) * 512],
                            start=(fc == 0), stop=(fc == 31)), deps=[ev_w2, h_evs[fc]] + war + [bw.get(("mm2", sub, half))])
                        bw[("mm2", sub, half)] = mm
                        mm_last = mm
                sub_evs.append(mm_last)
                bw["w2l"] = mm_last
                if sp == 1:
                    bw[("hf", fc)] = mm_last
            for si in range(2):
                sub = sp * 2 + si
                t_g = tb * 4 + sub
                for half in range(2):
                    bank = cx.pb[2 * si + half]
                    dd = [bw[("mm2", sub, half)], rep_evs[half], cx.ev_ceff]
                    y1 = p.op("vector", lambda v, bank=bank, half=half: v.tensor_tensor(fA, bank[:, 0:512], b2_rep[:, half * 512:(half + 1) * 512], ALU.add), deps=dd + [bw.get("fA2")])
                    y2 = p.op("vector", lambda v: v.tensor_scalar(fA, fA, 256.0, ce[:, 0:1], ALU.mult, ALU.add), deps=[y1])
                    y3 = p.op("vector", lambda v: v.tensor_scalar(fA, fA, ce[:, 0:1], 1.0 / 256.0, ALU.subtract, ALU.mult), deps=[y2])
                    e_x = p.op("vector", lambda v, t_g=t_g, half=half: v.tensor_tensor(
                        cx.x_sb[:, t_g, half * 512:(half + 1) * 512], cx.x_sb[:, t_g, half * 512:(half + 1) * 512],
                        fA, ALU.add), deps=[y3, x_upd[2 * t_g + half]])
                    bw[("o2", sub, half)] = e_x
                    bw["fA2"] = e_x
                    x2_upd.append(e_x)
        prev_w1 = sub_evs[-1:]
    return x2_upd


def emit_layer3(cx, stat_evs, xnf_evs):
    p, d = cx.p, cx.dram
    A1, A2, A3 = cx.AR1, cx.AR2, cx.AR3
    o_tm = A1[:, 0:16384].rearrange("p (t w) -> p t w", t=NT)
    wqkv = A2[:, 0:3072].rearrange("p (c w) -> p c w", c=NCH)
    qf = A2[:, 3072:5120]
    kf = A2[:, 5120:7168]
    vt = A2[:, 7168:8256].rearrange("p (t w) -> p t w", t=NT)
    vt2 = A2[:, 8256:9344].rearrange("p (t w) -> p t w", t=NT)
    pP = [A3[:, i * 512:(i + 1) * 512] for i in range(4)]
    sq_bf = A3[:, 2048:4096]
    negc = cx.small[:, 40:41]
    last_bank_user = cx.bw
    prev_use = list(stat_evs)

    ev_g = p.dma(cx.ch_m, lambda e: e.dma_start(out=cx.sA[:, 0:8], in_=d["l3_lng"]), deps=list(stat_evs))
    ev_b = p.dma(cx.ch_m, lambda e: e.dma_start(out=cx.sA[:, 8:16], in_=d["l3_lnb"]), deps=list(stat_evs))
    aff_evs = []
    for c in range(NCH):
        aff_evs.append(p.op("vector", lambda v, c=c: v.tensor_scalar(
            cx.xnf[:, c, :], cx.xnf[:, c, :], cx.sA[:, c:c + 1], cx.sA[:, 8 + c:9 + c],
            ALU.mult, ALU.add), deps=[ev_g, ev_b] + list(xnf_evs)))
    xnf_evs = aff_evs

    for hp in range(NCH):
        wevs = []
        for c in range(NCH):
            e_q = p.dma(cx.ch_wg, lambda e, c=c, hp=hp: e.dma_start(
                out=wqkv[:, c, 0:128], in_=d["l3_wq"][c * 128:(c + 1) * 128, hp * 128:(hp + 1) * 128]), deps=prev_use)
            e_k = p.dma(cx.ch_wg, lambda e, c=c, hp=hp: e.dma_start(
                out=wqkv[:, c, 128:256], in_=d["l3_wk"][c * 128:(c + 1) * 128, hp * 128:(hp + 1) * 128]), deps=prev_use)
            e_v = p.dma(cx.ch_wg, lambda e, c=c, hp=hp: e.dma_start(
                out=wqkv[:, c, 256:384], in_=d["l3_wv"][c * 128:(c + 1) * 128, hp * 128:(hp + 1) * 128]), deps=prev_use)
            wevs.append((e_q, e_k, e_v))

        qk_done, v_done = [], []
        for tq in range(4):
            bq, bk = cx.pb[tq % 2], cx.pb[2 + tq % 2]
            mq = mk = None
            for c in range(NCH):
                st_, sp = (c == 0), (c == NCH - 1)
                war_q = [last_bank_user.get(tq % 2)] if c == 0 else []
                war_k = [last_bank_user.get(2 + tq % 2)] if c == 0 else []
                mq = p.op("tensor", lambda tt, c=c, tq=tq, st_=st_, sp=sp, bq=bq: tt.matmul(
                    bq[:, 0:512], wqkv[:, c, 0:128], cx.xnf[:, c, tq * 512:(tq + 1) * 512], start=st_, stop=sp),
                    deps=[wevs[c][0]] + list(xnf_evs) + war_q)
                mk = p.op("tensor", lambda tt, c=c, tq=tq, st_=st_, sp=sp, bk=bk: tt.matmul(
                    bk[:, 0:512], wqkv[:, c, 128:256], cx.xnf[:, c, tq * 512:(tq + 1) * 512], start=st_, stop=sp),
                    deps=[wevs[c][1]] + war_k)
            eq = p.op("vector", lambda v, tq=tq, bq=bq: v.tensor_scalar(qf[:, tq * 512:(tq + 1) * 512], bq[:, 0:512], 1.0 / 8.0, None, ALU.mult), deps=[mq])
            ek = p.op("vector", lambda v, tq=tq, bk=bk: v.tensor_copy(kf[:, tq * 512:(tq + 1) * 512], bk[:, 0:512]), deps=[mk])
            last_bank_user[tq % 2] = eq
            last_bank_user[2 + tq % 2] = ek
            qk_done.append((eq, ek))
        for t in range(NT):
            bi = 4 + t % 2
            bank = cx.pb[bi]
            mv = None
            for c in range(NCH):
                st_, sp = (c == 0), (c == NCH - 1)
                war = [last_bank_user.get(bi)] if c == 0 else []
                mv = p.op("tensor", lambda tt, c=c, t=t, st_=st_, sp=sp, bank=bank: tt.matmul(
                    bank[:, 0:128], cx.xnf[:, c, t * 128:(t + 1) * 128], wqkv[:, c, 256:384], start=st_, stop=sp),
                    deps=[wevs[c][2]] + list(xnf_evs) + war)
            e1 = p.op("vector", lambda v, t=t, bank=bank: v.tensor_copy(vt[:, t, 0:64], bank[:, 0:64]), deps=[mv])
            e2 = p.op("vector", lambda v, t=t, bank=bank: v.tensor_copy(vt2[:, t, 0:64], bank[:, 64:128]), deps=[mv, e1])
            e3 = p.op("vector", lambda v, t=t: v.tensor_scalar(vt[:, t, 64:65], cx.ones_f[:, 0:1], 0.0, 1.0, ALU.mult, ALU.add), deps=[cx.ev_ones] + prev_use)
            e4 = p.op("vector", lambda v, t=t: v.tensor_scalar(vt2[:, t, 64:65], cx.ones_f[:, 0:1], 0.0, 1.0, ALU.mult, ALU.add), deps=[cx.ev_ones] + prev_use)
            last_bank_user[bi] = e2
            v_done.append((e1, e2, e3, e4))

        nr = cx.small
        e_sq = p.op("scalar", lambda s: s.activation(sq_bf[:, 0:T], qf[:, :], AF.Square), deps=[e for e, _ in qk_done] + prev_use)
        nevs = []
        for part, e_src in ((0, e_sq), (1, None)):
            if part == 1:
                e_src = p.op("scalar", lambda s: s.activation(sq_bf[:, 0:T], kf[:, :], AF.Square), deps=[e for _, e in qk_done] + [e for e, _ in nevs])
            for hh in range(2):
                for tq in range(4):
                    bank = cx.pb[0]
                    war = [last_bank_user.get(0)] if not nevs else [nevs[-1][1]]
                    em = p.op("tensor", lambda tt, hh=hh, tq=tq, bank=bank: tt.matmul(
                        bank[0:1, 0:512], cx.blk_bf[:, hh:hh + 1],
                        sq_bf[:, tq * 512:(tq + 1) * 512], start=True, stop=True),
                        deps=[e_src, cx.ev_blk] + war)
                    er = p.op("vector", lambda v, part=part, hh=hh, tq=tq, bank=bank: v.tensor_reduce(
                        nr[0:1, 8 + part * 8 + 4 * hh + tq:9 + part * 8 + 4 * hh + tq], bank[0:1, 0:512], axis=AX.X, op=ALU.max), deps=[em])
                    nevs.append((em, er))
        last_bank_user[0] = nevs[-1][1]
        dep_nr = [er for _, er in nevs]
        e_q2 = p.op("vector", lambda v: v.tensor_reduce(nr[0:1, 35:36], nr[0:1, 8:16], axis=AX.X, op=ALU.max), deps=dep_nr)
        e_k2 = p.op("vector", lambda v: v.tensor_reduce(nr[0:1, 36:37], nr[0:1, 16:24], axis=AX.X, op=ALU.max), deps=dep_nr)
        e_sq2 = p.op("scalar", lambda s: s.activation(nr[0:1, 35:36], nr[0:1, 35:36], AF.Sqrt), deps=[e_q2])
        e_sk2 = p.op("scalar", lambda s: s.activation(nr[0:1, 36:37], nr[0:1, 36:37], AF.Sqrt), deps=[e_k2])
        e_c1 = p.op("vector", lambda v: v.tensor_tensor(nr[0:1, 37:38], nr[0:1, 35:36], nr[0:1, 36:37], ALU.mult), deps=[e_sq2, e_sk2])
        e_c5 = p.op("vector", lambda v: v.tensor_scalar(nr[0:1, 39:40], nr[0:1, 37:38], -1.02, None, ALU.mult), deps=[e_c1])
        nc_bf = A3[0:1, 2049:2050]
        e_c6 = p.op("vector", lambda v: v.tensor_copy(nc_bf, nr[0:1, 39:40]), deps=[e_c5])
        e_c7 = p.op("tensor", lambda tt: tt.transpose(
            cx.pbt[1][:, 0:1], nc_bf.broadcast_to((1, 128)), cx.eye_bf[0:1, 0:1]),
            deps=[e_c6, cx.ev_eye, last_bank_user.get(("t", 1))])
        e_negc = p.op("vector", lambda v: v.tensor_copy(negc, cx.pbt[1][:, 0:1]), deps=[e_c7])
        last_bank_user[("t", 1)] = e_negc

        exp_by_buf = {}
        pv_last_for_buf = {}
        po_banks = (cx.pb[4], cx.pb[5])
        for head in range(2):
            kbase = 64 * head
            hgl = 2 * hp + head
            po = po_banks[head]
            vtile = vt if head == 0 else vt2
            for qg in range(4):
                n_kt = NT
                pv_evs = []
                for kt in range(n_kt):
                    bi = 2 * head + (kt % 2)
                    lb = cx.pb[bi][:, 0:512]
                    pbuf = pP[2 * head + (kt % 2)]
                    war_l = [exp_by_buf.get(bi), last_bank_user.get(bi)]
                    m1 = p.op("tensor", lambda tt, kt=kt, qg=qg, lb=lb, kbase=kbase: tt.matmul(
                        lb, kf[kbase:kbase + 64, kt * 128:(kt + 1) * 128],
                        qf[kbase:kbase + 64, qg * 512:(qg + 1) * 512], start=True, stop=True),
                        deps=[qk_done[-1][0], qk_done[-1][1]] + war_l)
                    war_p = [pv_last_for_buf.get(2 * head + (kt % 2))]
                    e_exp = p.op("scalar", lambda s, lb=lb, pbuf=pbuf: s.activation(
                        pbuf, lb, AF.Exp, bias=negc[:, 0:1]), deps=[m1, e_negc] + war_p)
                    exp_by_buf[bi] = e_exp
                    for sub in range(4):
                        st_ = (kt == 0 and sub == 0)
                        sp = (kt == n_kt - 1)
                        dep_po = [last_bank_user.get(4 + head)] if kt == 0 else []
                        mm = p.op("tensor", lambda tt, pbuf=pbuf, sub=sub, kt=kt, vtile=vtile, po=po, st_=st_, sp=sp: tt.matmul(
                            po[:, sub * 68:sub * 68 + 65], pbuf[:, sub * 128:(sub + 1) * 128],
                            vtile[:, kt, 0:65], start=st_, stop=sp),
                            deps=[e_exp, v_done[kt][0], v_done[kt][1], v_done[kt][2], v_done[kt][3]] + dep_po)
                        pv_evs.append(mm)
                    pv_last_for_buf[2 * head + (kt % 2)] = pv_evs[-1]
                e_prev = None
                for sub in range(4):
                    tq_idx = qg * 4 + sub
                    dn = cx.small[:, 42 + head * 2:43 + head * 2]
                    e_g = p.op("vector", lambda v, po=po, sub=sub, dn=dn: v.tensor_scalar(
                        dn, po[:, sub * 68 + 64:sub * 68 + 65], 1e-38, None, ALU.max), deps=pv_evs[-4:] + ([e_prev] if e_prev else []))
                    e_rc = p.op("vector", lambda v, dn=dn: v.reciprocal(dn, dn), deps=[e_g])
                    e_o = p.op("vector", lambda v, po=po, sub=sub, tq_idx=tq_idx, hgl=hgl, dn=dn: v.tensor_scalar(
                        o_tm[:, tq_idx, hgl * 64:(hgl + 1) * 64], po[:, sub * 68:sub * 68 + 64], dn[:, 0:1], None, ALU.mult), deps=[e_rc])
                    e_prev = e_o
                last_bank_user[4 + head] = e_prev
        for bi in range(4):
            if exp_by_buf.get(bi) is not None:
                last_bank_user[bi] = exp_by_buf[bi]
        prev_use = [last_bank_user[4], last_bank_user[5], exp_by_buf.get(0), exp_by_buf.get(1), exp_by_buf.get(2), exp_by_buf.get(3)]
    o_done = prev_use

    xs_all = cx.small[:, 48:64]
    xq_evs = []
    war = list(o_done)
    for t in range(NT):
        xn = cx.sA[:, 0:D]
        e_xn = p.op("vector", lambda v, t=t: v.tensor_scalar(
            xn, cx.x_sb[:, t, :], cx.ln_stats[:, 2 * t:2 * t + 1], cx.ln_stats[:, 2 * t + 1:2 * t + 2],
            ALU.mult, ALU.add), deps=[stat_evs[t]] + war)
        am = cx.small[:, 44:45]
        ri = cx.small[:, 45:46]
        e_am = p.op("vector", lambda v, am=am, xn=xn: v.tensor_reduce(am, xn, axis=AX.X, op=ALU.max, apply_absolute_value=True), deps=[e_xn])
        e_xs = p.op("vector", lambda v, am=am, t=t: v.tensor_scalar(xs_all[:, t:t + 1], am, 1e-6, 1.0 / 127.0, ALU.max, ALU.mult), deps=[e_am])
        e_ri = p.op("vector", lambda v, ri=ri, t=t: v.reciprocal(ri, xs_all[:, t:t + 1]), deps=[e_xs])
        xq_bf = A3[:, 0:1024]
        e_q1 = p.op("vector", lambda v, xn=xn, ri=ri: v.tensor_scalar(xn, xn, ri[:, 0:1], cx.ceff[:, 0:1], ALU.mult, ALU.add), deps=[e_ri])
        e_q2 = p.op("vector", lambda v, xn=xn: v.tensor_scalar(xn, xn, cx.ceff[:, 0:1], None, ALU.subtract), deps=[e_q1])
        e_q3 = p.op("vector", lambda v, xn=xn: v.tensor_scalar(xn, xn, 127.0, -128.0, ALU.min, ALU.max), deps=[e_q2])
        e_q = p.op("vector", lambda v, xq_bf=xq_bf, xn=xn: v.tensor_copy(xq_bf, xn), deps=[e_q3])
        tevs = []
        for c in range(NCH):
            bank = cx.pbt[c % 2]
            war_b = [last_bank_user.get(("t", c % 2))] if c < 2 else [tevs[-2][1]]
            e_t = p.op("tensor", lambda tt, c=c, bank=bank, xq_bf=xq_bf: tt.transpose(
                bank[:, 0:128], xq_bf[:, c * 128:(c + 1) * 128], cx.eye_bf[:, :]),
                deps=[e_q, cx.ev_eye] + war_b)
            e_c = p.op("vector", lambda v, t=t, c=c, bank=bank: v.tensor_copy(
                cx.xnf[:, c, t * 128:(t + 1) * 128], bank[:, 0:128]), deps=[e_t])
            tevs.append((e_t, e_c))
        last_bank_user[("t", 0)] = tevs[-2][1]
        last_bank_user[("t", 1)] = tevs[-1][1]
        xq_evs.append(tevs[-1][1])
        war = [e_q, tevs[-1][1], tevs[-2][1]]
    xq_all = list(xq_evs)

    xadd_evs = []

    wo_sb = A3[:, 4608:12800].rearrange("p (c w) -> p c w", c=NCH)
    wo_evs = [p.dma(cx.ch_wg, lambda e, c=c: e.dma_start(
        out=wo_sb[:, c, :], in_=d["l3_wo"][c * 128:(c + 1) * 128, :]), deps=o_done) for c in range(NCH)]
    ones_row = A3[:, 0:128]
    bo_bf = A3[0:1, 384:1408]
    ev_bo = p.dma(cx.ch_wg, lambda e: e.dma_start(out=bo_bf, in_=d["l3_bo"]), deps=xq_all[-1:])
    e_or = p.op("vector", lambda v: v.tensor_scalar(ones_row[0:1, 0:128], cx.ones_f[0:1, 0:1].broadcast_to((1, 128)), 1.0, None, ALU.mult), deps=xq_all[-1:] + [cx.ev_ones])

    stg = [A3[:, 128 + i * 128:128 + (i + 1) * 128] for i in range(2)]
    x_upd = []
    for qt in (range(NT) if "l3noproj" not in STAGES else []):
        b_out = (cx.pb[4], cx.pb[5])
        mm_last = [None, None]
        for c in range(NCH):
            bank = cx.pbt[c % 2]
            war_b = [last_bank_user.get(("t", c % 2))]
            e_t = p.op("tensor", lambda tt, qt=qt, c=c, bank=bank: tt.transpose(
                bank[:, 0:128], o_tm[:, qt, c * 128:(c + 1) * 128], cx.eye_bf[:, :]),
                deps=o_done + war_b + [cx.ev_eye])
            e_s = p.op("vector", lambda v, c=c, bank=bank: v.tensor_copy(stg[c % 2][:, 0:128], bank[:, 0:128]),
                       deps=[e_t] + ([mm_last[0], mm_last[1]] if c >= 2 else [last_bank_user.get(("stg3", c % 2)), xq_all[-1] if xq_all else None]))
            last_bank_user[("t", c % 2)] = e_s
            for half in range(2):
                st_ = (c == 0)
                war_o = [last_bank_user.get(4 + half)] if c == 0 else []
                mm = p.op("tensor", lambda tt, c=c, half=half, qt=qt, st_=st_: tt.matmul(
                    b_out[half][:, 0:512], stg[c % 2][:, 0:128], wo_sb[:, c, half * 512:(half + 1) * 512],
                    start=st_, stop=False), deps=[e_s, wo_evs[c]] + war_o)
                mm_last[half] = mm
            last_bank_user[("stg3", c % 2)] = mm_last[1]
        for half in range(2):
            mm = p.op("tensor", lambda tt, half=half: tt.matmul(
                b_out[half][:, 0:512], ones_row[0:1, 0:128], bo_bf[0:1, half * 512:(half + 1) * 512],
                start=False, stop=True), deps=[mm_last[half], e_or, ev_bo])
            mm_last[half] = mm
        for half in range(2):
            e_x = p.op("vector", lambda v, qt=qt, half=half: v.tensor_tensor(
                cx.x_sb[:, qt, half * 512:(half + 1) * 512], cx.x_sb[:, qt, half * 512:(half + 1) * 512],
                b_out[half][:, 0:512], ALU.add), deps=[mm_last[half]] + (xadd_evs[qt:qt+1] if xadd_evs else []))
            last_bank_user[4 + half] = e_x
            x_upd.append(e_x)

    if "l3noffn" in STAGES:
        return (x_upd if x_upd else (xadd_evs if xadd_evs else xq_all))
    wff = A1[:, 0:8192].rearrange("p (c w) -> p c w", c=NCH)
    base_done = list(x_upd) if x_upd else (list(xadd_evs) if xadd_evs else list(xq_all))
    wff_evs = [p.dma(cx.ch_wg, lambda e, c=c: e.dma_start(
        out=wff[:, c, :], in_=d["l3_wff"][c * 128:(c + 1) * 128, :]), deps=base_done) for c in range(NCH)]
    ws_bf = A3[0:1, 1408:2432]
    eb1 = p.dma(cx.ch_wg, lambda e: e.dma_start(out=ws_bf, in_=d["l3_ws"]), deps=base_done)
    bf_bf = A3[0:1, 2432:3456]
    eb2 = p.dma(cx.ch_wg, lambda e: e.dma_start(out=bf_bf, in_=d["l3_bff"]), deps=base_done)
    ws_rep = A2[:, 9344:11392].bitcast(F32)
    b_rep = A2[:, 11392:13440].bitcast(F32)
    rep_evs = []
    for (srcb, dst) in ((ws_bf, ws_rep), (bf_bf, b_rep)):
        for half in range(2):
            bank = cx.pb[0]
            war = [last_bank_user.get(0)] + ([rep_evs[-1]] if rep_evs else [])
            em = p.op("tensor", lambda tt, srcb=srcb, half=half, bank=bank: tt.matmul(
                bank[:, 0:512], ones_row[0:1, 0:128], srcb[0:1, half * 512:(half + 1) * 512],
                start=True, stop=True), deps=[eb1, eb2, e_or] + war)
            ec = p.op("vector", lambda v, dst=dst, half=half, bank=bank: v.tensor_copy(
                dst[:, half * 512:(half + 1) * 512], bank[:, 0:512]), deps=[em])
            rep_evs.append(ec)
    last_bank_user[0] = rep_evs[-1]
    if "l3f" in DBG:
        dws = cx.nc.dram_tensor("y_wsrep", [PT, 2048], F32, kind="ExternalOutput").ap()
        cx.dbg_evs.append(p.dma(cx.ch_m, lambda e: e.dma_start(out=dws[:, 0:1024], in_=ws_rep), deps=rep_evs))
        cx.dbg_evs.append(p.dma(cx.ch_m, lambda e: e.dma_start(out=dws[:, 1024:2048], in_=b_rep), deps=rep_evs))
        dwf = cx.nc.dram_tensor("y_wff", [PT, 1024], BF16, kind="ExternalOutput").ap()
        cx.dbg_evs.append(p.dma(cx.ch_m, lambda e: e.dma_start(out=dwf, in_=wff[:, 0, :]), deps=wff_evs))
        dxq3 = cx.nc.dram_tensor("y_xq3", [PT, NCH * T], BF16, kind="ExternalOutput").ap()
        cx.dbg_evs.append(p.dma(cx.ch_m, lambda e: e.dma_start(out=dxq3, in_=cx.xnf[:, :, :].rearrange("p c t -> p (c t)")), deps=xq_all))

    ffn_upd = []
    for t in range(NT):
        md = [None, None]
        for c in range(NCH):
            st_, sp = (c == 0), (c == NCH - 1)
            for half in range(2):
                war_o = [last_bank_user.get(2 + half)] if c == 0 else []
                md[half] = p.op("tensor", lambda tt, c=c, t=t, half=half, st_=st_, sp=sp: tt.matmul(
                    cx.pb[2 + half][:, 0:512], cx.xnf[:, c, t * 128:(t + 1) * 128],
                    wff[:, c, half * 512:(half + 1) * 512], start=st_, stop=sp),
                    deps=[wff_evs[c]] + (xq_all if c == 0 else []) + war_o)
        if "l3acc" in DBG and t == 1:
            dacc = cx.nc.dram_tensor("y_acc", [PT, 512], F32, kind="ExternalOutput").ap()
            e_ca = p.op("vector", lambda v: v.tensor_copy(cx.sC[:, 0:512], cx.pb[2][:, 0:512]), deps=[md[0]])
            cx.dbg_evs.append(p.dma(cx.ch_m, lambda e: e.dma_start(out=dacc, in_=cx.sC[:, 0:512]), deps=[e_ca]))
        for half in range(2):
            sl = cx.sA[:, 0:512]
            e1 = p.op("vector", lambda v, t=t, half=half, sl=sl: v.tensor_scalar(
                sl, cx.pb[2 + half][:, 0:512], xs_all[:, t:t + 1], None, ALU.mult), deps=[md[half]] + ([ffn_upd[-1]] if ffn_upd else [last_bank_user.get("sA3")]) + ([e_ca] if ("l3acc" in DBG and t == 1) else []))
            e2 = p.op("vector", lambda v, half=half, sl=sl: v.tensor_tensor(
                sl, sl, ws_rep[:, half * 512:(half + 1) * 512], ALU.mult), deps=[e1, rep_evs[1]])
            e3 = p.op("vector", lambda v, half=half, sl=sl: v.tensor_tensor(
                sl, sl, b_rep[:, half * 512:(half + 1) * 512], ALU.add), deps=[e2, rep_evs[3]])
            if "l3noxn" not in STAGES:
                sx = cx.sC[:, 512:1024]
                e3a = p.op("vector", lambda v, t=t, half=half, sx=sx: v.tensor_scalar(
                    sx, cx.x_sb[:, t, half * 512:(half + 1) * 512],
                    cx.ln_stats[:, 2 * t:2 * t + 1], cx.ln_stats[:, 2 * t + 1:2 * t + 2],
                    ALU.mult, ALU.add), deps=[stat_evs[t], e3])
                e3 = p.op("vector", lambda v, sl=sl, sx=sx: v.tensor_tensor(sl, sl, sx, ALU.add), deps=[e3a])
            e4 = p.op("vector", lambda v, t=t, half=half, sl=sl: v.tensor_tensor(
                cx.x_sb[:, t, half * 512:(half + 1) * 512], cx.x_sb[:, t, half * 512:(half + 1) * 512],
                sl, ALU.add), deps=[e3])
            last_bank_user[2 + half] = e4
            last_bank_user["sA3"] = e4
            ffn_upd.append(e4)
    return ffn_upd


def _get_program(n_layers):
    if n_layers not in _PROG_CACHE:
        cx = build_program(n_layers)
        nc = cx.nc
        esem, csem = {}, []
        sem_stack = []
        for e in ENGINES:
            h = nc.semaphore(f"sem_{e}")
            sem_stack.append(h)
            esem[e] = h.__enter__()
        for ci in range(len(cx.p.channels)):
            h = nc.semaphore(f"sem_ch{ci}")
            sem_stack.append(h)
            csem.append(h.__enter__())
        with nc.Block() as block:
            cx.p.emit(block, esem, csem)
        for h in reversed(sem_stack):
            h.__exit__(None, None, None)
        cx.stack.close()
        _PROG_CACHE[n_layers] = nc
    return _PROG_CACHE[n_layers]


def host_consts():
    eye = np.eye(128, dtype=np.float32)
    J = eye[::-1].copy()
    blk = np.zeros((128, 2), np.float32)
    blk[0:64, 0] = 1.0
    blk[64:128, 1] = 1.0
    ones = np.ones((128, 16), np.float32)
    wline = np.full((1, 2048), NEG_BIG, np.float32)
    d = np.arange(2048) - 511
    wline[0, (d >= 0) & (d <= 255)] = 0.0
    mline = np.full((1, 6144), NEG_BIG, np.float32)
    mline[0, 4064:] = 0.0
    ij = np.arange(128)
    triu = (ij[:, None] <= ij[None, :]).astype(np.float32)
    one128 = np.ones((128, 128), np.float32)
    msk = np.zeros((128, 256), np.uint8)
    msk[:, 0:128] = (ij[:, None] < ij[None, :]).astype(np.uint8)
    msk[:, 128:256] = (ij[:, None] <= ij[None, :]).astype(np.uint8)
    return {"c_eye": eye, "c_J": J, "c_blk": blk, "c_ones": ones,
            "c_wline": wline, "c_mline": mline, "c_triu": triu,
            "c_one128": one128, "c_msk": msk}


def kernel(**inputs):
    nc = _get_program(N_LAYERS)
    consts = host_consts()
    xfull = np.ascontiguousarray(np.asarray(inputs["L0_attn_x"], dtype=np.float32))
    base = dict(consts)
    base["l0_wq"] = np.ascontiguousarray(np.asarray(inputs["L0_attn_Wq"], np.float32))
    base["l0_wk"] = np.ascontiguousarray(np.asarray(inputs["L0_attn_Wk"], np.float32))
    base["l0_wv"] = np.ascontiguousarray(np.asarray(inputs["L0_attn_Wv"], np.float32))
    base["l0_wo"] = np.ascontiguousarray(np.asarray(inputs["L0_attn_Wo"], np.float32))
    base["l0_bo"] = np.ascontiguousarray(np.asarray(inputs["L0_attn_bo"], np.float32).reshape(1, D))
    base["l0_rpb"] = np.ascontiguousarray(np.asarray(inputs["L0_attn_rel_pos_bias"], np.float32))
    base["l0_wg"] = np.ascontiguousarray(np.asarray(inputs["L0_ffn_w_gate"], np.float32))
    base["l0_wu"] = np.ascontiguousarray(np.asarray(inputs["L0_ffn_w_up"], np.float32))
    base["l0_wd"] = np.ascontiguousarray(np.asarray(inputs["L0_ffn_w_down"], np.float32))
    base["l1_wq"] = np.ascontiguousarray(np.asarray(inputs["L1_attn_wq"], np.float32))
    base["l1_wk"] = np.ascontiguousarray(np.asarray(inputs["L1_attn_wk"], np.float32))
    base["l1_wv"] = np.ascontiguousarray(np.asarray(inputs["L1_attn_wv"], np.float32))
    base["l1_wout"] = np.ascontiguousarray(np.asarray(inputs["L1_attn_w_out"], np.float32))
    base["l1_wgq"] = np.ascontiguousarray(np.asarray(inputs["L1_attn_wgq"], np.float32))
    base["l1_wgk"] = np.ascontiguousarray(np.asarray(inputs["L1_attn_wgk"], np.float32))
    base["l1_wgv"] = np.ascontiguousarray(np.asarray(inputs["L1_attn_wgv"], np.float32))
    base["l1_rout"] = np.ascontiguousarray(np.asarray(inputs["L1_attn_r_out_w"], np.float32))
    base["l1_glw"] = np.ascontiguousarray(np.stack(
        [np.asarray(inputs["L1_attn_gate_local_w"], np.float32).reshape(D),
         np.asarray(inputs["L1_attn_gate_remote_w"], np.float32).reshape(D)], axis=1))
    base["l1_wg"] = np.ascontiguousarray(np.asarray(inputs["L1_ffn_w_gate"], np.float32))
    base["l1_wu"] = np.ascontiguousarray(np.asarray(inputs["L1_ffn_w_up"], np.float32))
    base["l1_wd"] = np.ascontiguousarray(np.asarray(inputs["L1_ffn_w_down"], np.float32))
    base["l2_wq"] = np.ascontiguousarray(np.asarray(inputs["L2_attn_Wq"], np.float32).T)
    base["l2_wk"] = np.ascontiguousarray(np.asarray(inputs["L2_attn_Wk"], np.float32).T)
    base["l2_wv"] = np.ascontiguousarray(np.asarray(inputs["L2_attn_Wv"], np.float32).T)
    base["l2_wa"] = np.ascontiguousarray(np.asarray(inputs["L2_attn_Wa"], np.float32).T)
    base["l2_war"] = np.ascontiguousarray(np.asarray(inputs["L2_attn_WaR"], np.float32).T)
    base["l2_wo"] = np.ascontiguousarray(np.asarray(inputs["L2_attn_Wo"], np.float32).T)
    base["l2_wb"] = np.ascontiguousarray(np.asarray(inputs["L2_attn_Wb"], np.float32).T)
    base["l2_wg"] = np.ascontiguousarray(np.asarray(inputs["L2_attn_Wg"], np.float32).T)
    base["l2_ba"] = np.ascontiguousarray(np.asarray(inputs["L2_attn_ba"], np.float32).reshape(8, 128).T)
    base["l2_bar"] = np.ascontiguousarray(np.asarray(inputs["L2_attn_baR"], np.float32).reshape(8, 128).T)
    base["l2_bb"] = np.ascontiguousarray(np.asarray(inputs["L2_attn_bb"], np.float32).reshape(16, 1))
    base["l2_bg"] = np.ascontiguousarray(np.asarray(inputs["L2_attn_bg"], np.float32).reshape(16, 1))
    base["l2_lnw"] = np.ascontiguousarray(np.asarray(inputs["L2_attn_ln_w"], np.float32).reshape(8, 128).T)
    base["l2_lnb"] = np.ascontiguousarray(np.asarray(inputs["L2_attn_ln_b"], np.float32).reshape(8, 128).T)
    base["l2_w1"] = np.ascontiguousarray(np.asarray(inputs["L2_ffn_w1"], np.float32))
    base["l2_b1"] = np.ascontiguousarray(np.asarray(inputs["L2_ffn_b1"], np.float32).reshape(32, 128).T)
    base["l2_w2"] = np.ascontiguousarray(np.asarray(inputs["L2_ffn_w2"], np.float32))
    base["l2_b2"] = np.ascontiguousarray(np.asarray(inputs["L2_ffn_b2"], np.float32).reshape(1, D))
    wqkv3 = np.asarray(inputs["L3_attn_w_qkv"], np.float32).reshape(D, 16, 64, 3)
    base["l3_wq"] = np.ascontiguousarray(wqkv3[:, :, :, 0].reshape(D, D))
    base["l3_wk"] = np.ascontiguousarray(wqkv3[:, :, :, 1].reshape(D, D))
    base["l3_wv"] = np.ascontiguousarray(wqkv3[:, :, :, 2].reshape(D, D))
    base["l3_wo"] = np.ascontiguousarray(np.asarray(inputs["L3_attn_w_proj"], np.float32))
    base["l3_bo"] = np.ascontiguousarray(np.asarray(inputs["L3_attn_b_proj"], np.float32).reshape(1, D))
    base["l3_lng"] = np.ascontiguousarray(np.asarray(inputs["L3_attn_ln_scale"], np.float32).reshape(8, 128).T)
    base["l3_lnb"] = np.ascontiguousarray(np.asarray(inputs["L3_attn_ln_bias"], np.float32).reshape(8, 128).T)
    base["l3_wff"] = np.ascontiguousarray(np.asarray(inputs["L3_ffn_weight_q"], np.float32).T)
    base["l3_ws"] = np.ascontiguousarray(np.asarray(inputs["L3_ffn_weight_scale"], np.float32).reshape(1, D))
    base["l3_bff"] = np.ascontiguousarray(np.asarray(inputs["L3_ffn_bias"], np.float32).reshape(1, D))
    in_maps = []
    for c in range(NCORES):
        m = dict(base)
        m["x"] = np.ascontiguousarray(xfull[c])
        in_maps.append(m)
    res = run_bass_kernel_spmd(nc, in_maps, core_ids=list(range(NCORES)))
    out = np.stack([res.results[c]["y"] for c in range(NCORES)], axis=0)
    global LAST_EXTRA
    LAST_EXTRA = {k: np.asarray(v) for k, v in res.results[0].items() if k != "y"}
    return out
